# version 50
# baseline (speedup 1.0000x reference)
import numpy as np
import ml_dtypes
import concourse.bass as bass
import concourse.mybir as mybir
from concourse.bass_utils import run_bass_kernel_spmd

F32 = mybir.dt.float32
BF16 = mybir.dt.bfloat16
AF = mybir.ActivationFunctionType
ALU = mybir.AluOpType
AX = mybir.AxisListType

T = 2048
D = 1024
H = 16
FD = 16
HD = 64
VW = 65
E = 8
NT = 16
DC = 8
NU = 64
EPS = 1e-5
EPS_Z = 1e-10
TOTAL_INV = 1.0 / 32768.0

DEBUG = ()
STOP = None

ENGINES = ("sync", "tensor", "vector", "scalar", "gpsimd")


class Prog:

    def __init__(self):
        self.ops = {e: [] for e in ENGINES}
        self.cnt = {}
        self.pending_gate = None
        self.gate_done = set()

    def sem(self, name):
        self.cnt[name] = 0

    ENG_SEM = {"vector": "v", "scalar": "s", "gpsimd": "g"}

    def op(self, eng, fn, waits=(), inc=None, amt=1, dma=False):
        auto = inc is None and not dma and eng in self.ENG_SEM
        if auto:
            inc = self.ENG_SEM[eng]
        waits = list(waits)
        if auto and eng == "gpsimd" and self.cnt["g"] > 0:
            waits.append(("g", self.cnt["g"]))
        waits = [w for w in waits if w is not None]
        if self.pending_gate is not None and eng not in self.gate_done:
            waits.extend(self.pending_gate)
            self.gate_done.add(eng)
        for (s, v) in waits:
            assert v <= self.cnt[s], f"wait {s}>={v} but planned {self.cnt[s]}"
        ev = None
        if inc is not None:
            self.cnt[inc] += amt
            ev = (inc, self.cnt[inc])
        self.ops[eng].append((fn, tuple(waits), inc, amt))
        return ev

    def counts_event(self, *sems):
        return [(s, self.cnt[s]) for s in sems if self.cnt[s] > 0]

    def gate_after(self, evs):
        evs = [e for e in evs if e is not None]
        self.pending_gate = evs
        self.gate_done = set()


def _emit(engobj, oplist, semh):
    waited = {}
    for fn, waits, inc, amt in oplist:
        for (s, v) in waits:
            if waited.get(s, -1) < v:
                engobj.wait_ge(semh[s], v)
                waited[s] = v
        instr = fn(engobj)
        if inc is not None:
            instr.then_inc(semh[inc], amt)


def build(debug=(), stop=None, cc_mode="ar"):
    nc = bass.Bass("TRN2", target_bir_lowering=False, debug=False)
    p = Prog()
    for s in ("ldx", "ldx1", "ldx2", "ldx3",
              "ldw", "ldwv", "ldwv2", "ldg", "ldo", "ldo2", "ldo3", "ldo4",
              "ldc", "ldew0", "ldew1",
              "tq0", "tq1", "tq2", "tq3", "tp", "ldm", "tl0", "tl1",
              "ccs", "cca", "st", "std", "st2", "t", "v", "s", "g"):
        p.sem(s)

    d_x = nc.dram_tensor("x", [T, D], F32, kind="ExternalInput").ap()
    d_wqkv = nc.dram_tensor("wqkv", [D, 1536], F32, kind="ExternalInput").ap()
    d_wo = nc.dram_tensor("wo", [D, D], F32, kind="ExternalInput").ap()
    d_gw = nc.dram_tensor("gw", [D, E], F32, kind="ExternalInput").ap()
    d_ew = nc.dram_tensor("ew", [64, 128, 1024], F32, kind="ExternalInput").ap()
    d_eb = nc.dram_tensor("eb", [128, 64], F32, kind="ExternalInput").ap()
    d_gbr = nc.dram_tensor("gbr", [128, E], F32, kind="ExternalInput").ap()
    d_gmh = nc.dram_tensor("gmh", [128, 512], F32, kind="ExternalInput").ap()
    d_bth = nc.dram_tensor("bth", [128, 512], F32, kind="ExternalInput").ap()
    d_mask = nc.dram_tensor("maskc", [128, 128], BF16, kind="ExternalInput").ap()
    d_ident = nc.dram_tensor("identc", [128, 128], BF16, kind="ExternalInput").ap()
    d_onesc = nc.dram_tensor("onesc", [128, 1], F32, kind="ExternalInput").ap()
    d_onesr = nc.dram_tensor("onesr", [1, 128], F32, kind="ExternalInput").ap()
    d_out = nc.dram_tensor("out", [T, D], F32, kind="ExternalOutput").ap()
    d_ccin = nc.dram_tensor("cc_in", [1, E], F32, kind="Internal").ap()
    d_ccout = nc.dram_tensor("cc_out", [1, E], F32, kind="Internal",
                             addr_space="Shared").ap()
    d_dbg = {}
    for name, shape, dt in (
        ("dbg_ln1fm", [128, 16384], BF16), ("dbg_xnq", [128, 4096], BF16),
        ("dbg_xnk", [128, 4096], BF16), ("dbg_vtm", [128, 16640], BF16),
        ("dbg_ofm", [128, 16384], BF16), ("dbg_x1", [128, 16384], BF16),
        ("dbg_ln3fm", [128, 16384], BF16), ("dbg_cb", [128, 8], F32),
        ("dbg_ffnfm", [128, 16384], BF16), ("dbg_opair", [128, 4096], BF16),
        ("dbg_st0", [128, 130], BF16), ("dbg_oraw", [128, 65], F32),
        ("dbg_qffm", [128, 2048], BF16), ("dbg_stA", [128, 130], BF16),
        ("dbg_zr", [128, 1], F32), ("dbg_osl", [128, 64], F32),
    ):
        if name in debug:
            d_dbg[name] = nc.dram_tensor(name, shape, dt, kind="ExternalOutput").ap()

    def dbg(name, sbuf_ap, waits):
        if name in d_dbg:
            ev = p.op("sync", lambda sy, d=d_dbg[name], s0=sbuf_ap:
                      sy.dma_start(out=d, in_=s0), waits=list(waits),
                      inc="std", amt=16)
            p.gate_after([ev])

    def dbgng(name, sbuf_ap, waits):
        if name in d_dbg:
            p.op("sync", lambda sy, d=d_dbg[name], s0=sbuf_ap:
                 sy.dma_start(out=d, in_=s0), waits=list(waits),
                 inc="std", amt=16)

    def A(name, shape, dt):
        return nc.alloc_sbuf_tensor(name, shape, dt).ap()

    x_tm = A("x_tm", [128, NT * D], BF16)
    lnR = A("lnR", [128, 16384], BF16)
    wqkv = A("wqkv_sb", [128, 12288], BF16)
    wo = wqkv[:, 0:8192]
    gw = A("gw_sb", [128, 64], BF16)
    ewb = A("ew_sb", [128, 2048], BF16)
    xn_qk = A("xn_qk", [128, 8192], BF16)
    v_tm = A("v_tm", [128, NT * H * VW], BF16)
    qkf_tm = A("qkf_tm", [128, 8192], BF16)
    qkf_fm = A("qkf_fm", [128, 8192], BF16)
    opair = A("opair", [128, 4096], BF16)
    at_sb = A("at_sb", [128, 512], BF16)
    st_bf = A("state_bf", [128, 520], BF16)
    lnt = A("lnt", [128, 2048], BF16)
    ln4o = lnR[:, 0:2048]
    hsb = A("hs", [128, 4096], BF16)
    sqj = hsb

    f32s = A("f32s", [128, 1536], F32)
    stats = A("stats", [128, 256], F32)
    fmst = A("fmst", [128, 192], F32)
    gmh = A("gmh_sb", [128, 512], BF16)
    bth = A("bth_sb", [128, 512], BF16)
    gbr = A("gbr_sb", [128, E], F32)
    eb = A("eb_sb", [128, 64], F32)
    ebs = A("ebs_sb", [128, 64], F32)
    cb = A("cb_sb", [128, E], F32)
    onesc = A("onesc_sb", [128, 1], F32)
    onesr = A("onesr_sb", [1, 128], F32)
    cnt_l = A("cnt_l", [1, E], F32)
    cnt_g = A("cnt_g", [1, E], F32)
    gate_sb = A("gate_sb", [128, 16], F32)
    gsc = A("gsc", [128, 32], F32)
    zr = A("zr", [128, 8], F32)
    maskc = A("maskc_sb", [128, 128], BF16)
    identc = A("identc_sb", [128, 128], BF16)

    psum = nc.alloc_psum_tensor("psum", [128, 4096], F32).ap()

    def b(i, n=512):
        return psum[:, i * 512: i * 512 + n]

    def bb(i):
        return psum[:, i * 512: (i + 1) * 512].bitcast(BF16)

    ldxs = ["ldx", "ldx1", "ldx2", "ldx3"]
    ev_ldx = {}
    wq8 = wqkv.rearrange("p (c f) -> p c f", c=8)
    d_wq8 = d_wqkv.rearrange("(c p) f -> p c f", p=128)

    def ld_x(i):
        src = d_x[i * 512:(i + 1) * 512, :].rearrange("(t p) d -> p t d", p=128)
        dst = x_tm[:, i * 4096:(i + 1) * 4096].rearrange("p (t d) -> p t d", d=D)
        ev_ldx[i] = p.op("gpsimd", lambda g, s0=src, d0=dst: g.dma_start(
            out=d0, in_=s0), inc=ldxs[i], amt=16)

    ld_x(0)
    ev_wqkv = p.op("gpsimd", lambda g: g.dma_start(
        out=wq8[:, :, 0:512], in_=d_wq8[:, :, 0:512]), inc="ldw", amt=16)
    for dst, src in ((gmh, d_gmh), (bth, d_bth)):
        ev_consts2 = p.op("gpsimd", lambda g, d=dst, s0=src: g.dma_start(
            out=d, in_=s0), inc="ldm", amt=16)
    ev_wqkv_v = p.op("gpsimd", lambda g: g.dma_start(
        out=wq8[:, 0:4, 512:1536], in_=d_wq8[:, 0:4, 512:1536]),
        inc="ldwv", amt=16)
    ld_x(1)
    ev_wqkv_v2 = p.op("gpsimd", lambda g: g.dma_start(
        out=wq8[:, 4:8, 512:1536], in_=d_wq8[:, 4:8, 512:1536]),
        inc="ldwv2", amt=16)
    ld_x(2)
    ld_x(3)
    ev_ew = {}
    ev_consts = None
    for dst, src in ((gbr, d_gbr), (maskc, d_mask),
                     (identc, d_ident), (onesc, d_onesc), (onesr, d_onesr),
                     (eb, d_eb)):
        ev_consts = p.op("sync", lambda sy, d=dst, s0=src: sy.dma_start(
            out=d, in_=s0), inc="ldc", amt=16)
    ev_gm = p.op("vector", lambda v: v.tensor_scalar_mul(gmh, gmh, 0.5),
                 waits=[ev_consts, ("ldm", 32)])
    ev_gb = p.op("vector", lambda v: v.tensor_scalar_mul(bth, bth, 0.5),
                 waits=[ev_gm])

    sq_last = [None]

    def sq_war():
        return [sq_last[0]] if sq_last[0] is not None else []

    def emit_ln(ln_id, t, src_ap, out_ap, extra_waits=()):
        base = ln_id * 64
        s1 = stats[:, base + t: base + t + 1]
        s2 = stats[:, base + 16 + t: base + 17 + t]
        rstd = stats[:, base + 32 + t: base + 33 + t]
        nmr = stats[:, base + 48 + t: base + 49 + t]
        w = list(extra_waits)
        ev_r = p.op("vector", lambda v, s1=s1, sa=src_ap: v.tensor_reduce(
            s1, sa, AX.X, ALU.add), waits=w)
        wq_ = list(extra_waits) + list(sq_war()) + [ev_r]
        ev_q = p.op("scalar", lambda sc, s2=s2, sa=src_ap: sc.activation(
            sqj[:, 0:1024], sa, AF.Square, accum_out=s2), waits=wq_)
        sq_last[0] = ev_q
        ev_m = p.op("vector", lambda v, s1=s1: v.tensor_scalar_mul(
            s1, s1, 1.0 / D), waits=[ev_r])
        ev_va = p.op("vector", lambda v, rstd=rstd, s2=s2: v.tensor_scalar(
            rstd, s2, 1.0 / D, EPS, ALU.mult, ALU.add), waits=[ev_q])
        ev_n0 = p.op("vector", lambda v, nmr=nmr, s1=s1: v.scalar_tensor_tensor(
            nmr, s1, -1.0, s1, ALU.mult, ALU.mult), waits=[ev_m])
        ev_ad = p.op("vector", lambda v, rstd=rstd, nmr=nmr: v.tensor_tensor(
            rstd, rstd, nmr, ALU.add), waits=[ev_va, ev_n0])
        ev_rc = p.op("scalar", lambda sc, rstd=rstd: sc.activation(
            rstd, rstd, AF.Ln), waits=[ev_ad])
        ev_sq = p.op("scalar", lambda sc, rstd=rstd: sc.activation(
            rstd, rstd, AF.Exp, scale=-0.5), waits=[ev_rc])
        ev_nm = p.op("vector", lambda v, nmr=nmr, s1=s1, rstd=rstd:
                     v.scalar_tensor_tensor(nmr, s1, -1.0, rstd, ALU.mult,
                                            ALU.mult), waits=[ev_sq, ev_n0])
        ev_ap = p.op("scalar", lambda sc, o=out_ap, sa=src_ap, rstd=rstd, nmr=nmr:
                     sc.activation(o, sa, AF.Identity, scale=rstd, bias=nmr),
                     waits=[ev_nm])
        return ev_ap

    def emit_ln_batch(ln_id, src16, extra_waits=()):
        base = ln_id * 64
        s1a = stats[:, base:base + 16]
        s2a = stats[:, base + 16:base + 32]
        rstda = stats[:, base + 32:base + 48]
        nmra = stats[:, base + 48:base + 64]
        w = list(extra_waits)
        src3 = src16.rearrange("p (t d) -> p t d", d=D)
        ev_r = p.op("vector", lambda v: v.tensor_reduce(
            s1a, src3, AX.X, ALU.add), waits=w)
        ev_q = None
        for t in range(NT):
            wq_ = list(w) + list(sq_war()) if t == 0 else []
            ev_q = p.op("scalar", lambda sc, t=t: sc.activation(
                sqj[:, 0:1024], src16[:, t * D:(t + 1) * D], AF.Square,
                accum_out=s2a[:, t:t + 1]),
                waits=wq_ + ([ev_q] if ev_q else []))
        sq_last[0] = ev_q
        ev_m = p.op("vector", lambda v: v.tensor_scalar_mul(
            s1a, s1a, 1.0 / D), waits=[ev_r])
        ev_va = p.op("vector", lambda v: v.tensor_scalar(
            rstda, s2a, 1.0 / D, EPS, ALU.mult, ALU.add), waits=[ev_q])
        ev_n0 = p.op("vector", lambda v: v.scalar_tensor_tensor(
            nmra, s1a, -1.0, s1a, ALU.mult, ALU.mult), waits=[ev_m])
        ev_ad = p.op("vector", lambda v: v.tensor_tensor(
            rstda, rstda, nmra, ALU.add), waits=[ev_va, ev_n0])
        ev_rc = p.op("scalar", lambda sc: sc.activation(
            rstda, rstda, AF.Ln), waits=[ev_ad])
        ev_sq = p.op("scalar", lambda sc: sc.activation(
            rstda, rstda, AF.Exp, scale=-0.5), waits=[ev_rc])
        ev_nm = p.op("vector", lambda v: v.scalar_tensor_tensor(
            nmra, s1a, -1.0, rstda, ALU.mult, ALU.mult), waits=[ev_sq, ev_n0])
        return rstda, nmra, ev_nm

    ev_qkevac = {}
    ev_v = {}
    ev_xn = {}
    ev_fm_ap = {}
    ev_trdma = {}
    ev_s2red = {}
    ev_qkmm = {}
    ev_vmm = {}
    s1g = stats[:, 0:16]
    s2g = stats[:, 16:32]
    rstd1 = stats[:, 32:48]
    nmr1 = stats[:, 48:64]
    tail1 = {}
    ev_q1 = {}
    ev_r1 = {}

    def emit_ln1_stats(g):
        src3g = x_tm[:, g * 4096:(g + 1) * 4096].rearrange(
            "p (t d) -> p t d", d=D)
        ev_r1[g] = p.op("vector", lambda v, g=g, src3g=src3g: v.tensor_reduce(
            s1g[:, g * 4:g * 4 + 4], src3g, AX.X, ALU.add),
            waits=[(ldxs[g], 16)])
        for t in range(g * 4, g * 4 + 4):
            qw = [(ldxs[g], 16)] + list(sq_war())
            ev_q1[t] = p.op("scalar", lambda sc, t=t: sc.activation(
                sqj[:, 0:1024], x_tm[:, t * D:(t + 1) * D], AF.Square,
                accum_out=s2g[:, t:t + 1]), waits=qw)
            sq_last[0] = ev_q1[t]
        c0 = g * 4
        s1h = s1g[:, c0:c0 + 4]
        s2h = s2g[:, c0:c0 + 4]
        rsh = rstd1[:, c0:c0 + 4]
        nmh = nmr1[:, c0:c0 + 4]
        ev_m = p.op("vector", lambda v, s1h=s1h: v.tensor_scalar_mul(
            s1h, s1h, 1.0 / D), waits=[ev_r1[g]])
        ev_va = p.op("vector", lambda v, rsh=rsh, s2h=s2h: v.tensor_scalar(
            rsh, s2h, 1.0 / D, EPS, ALU.mult, ALU.add), waits=[ev_q1[c0 + 3]])
        ev_n0 = p.op("vector", lambda v, nmh=nmh, s1h=s1h:
                     v.scalar_tensor_tensor(
            nmh, s1h, -1.0, s1h, ALU.mult, ALU.mult), waits=[ev_m])
        ev_ad = p.op("vector", lambda v, rsh=rsh, nmh=nmh: v.tensor_tensor(
            rsh, rsh, nmh, ALU.add), waits=[ev_va, ev_n0])
        ev_rc = p.op("scalar", lambda sc, rsh=rsh: sc.activation(
            rsh, rsh, AF.Ln), waits=[ev_ad])
        ev_sq = p.op("scalar", lambda sc, rsh=rsh: sc.activation(
            rsh, rsh, AF.Exp, scale=-0.5), waits=[ev_rc])
        tail1[g] = p.op("vector", lambda v, nmh=nmh, s1h=s1h, rsh=rsh:
                        v.scalar_tensor_tensor(nmh, s1h, -1.0, rsh,
                                               ALU.mult, ALU.mult),
                        waits=[ev_sq, ev_n0])

    ld_gate = [("ldx", 16), ("ldx1", 16), ("ldx2", 16), ("ldx3", 16),
               ("ldw", 16), ("ldwv", 16), ("ldwv2", 16), ("ldm", 32),
               ("ldc", p.cnt["ldc"])]
    ev_trP1 = {}

    def emit_ln_tr(t):
        if t // 4 not in tail1:
            emit_ln1_stats(t // 4)
        xt = x_tm[:, t * D:(t + 1) * D]
        ls = (t % 2) * 1024
        lslot = lnt[:, ls:ls + 1024]
        w = [tail1[t // 4]]
        if t >= 2:
            w.append(ev_trP1[t - 2])
        ev_ap = p.op("scalar", lambda sc, o=lslot, sa=xt, t=t: sc.activation(
            o, sa, AF.Identity, scale=rstd1[:, t:t + 1],
            bias=nmr1[:, t:t + 1]), waits=w)
        tb = 3 + (t % 2)
        trw = [ev_ap]
        if t == 0:
            trw.append(ev_consts)
        if t >= 2:
            trw.append(ev_trdma[t - 2])
        for dc in range(DC):
            ev_trP1[t] = p.op("tensor", lambda te, dc=dc, ls=ls, tb=tb:
                              te.transpose(
                bb(tb)[:, dc * 128:(dc + 1) * 128],
                in_=lnt[:, ls + dc * 128: ls + (dc + 1) * 128],
                identity=identc),
                waits=trw if dc == 0 else (),
                inc="t" if dc == DC - 1 else None)
    def emit_tr_evac(t):
        tb = 3 + (t % 2)
        dstv = lnR.rearrange("p (c n) -> p c n", c=8)[:, :, t * 128:(t + 1) * 128]
        srcv = bb(tb).rearrange("p (c n) -> p c n", c=8)
        ev_trdma[t] = p.op("vector", lambda v, d=dstv, s0=srcv:
                           v.tensor_copy(d, s0), waits=[ev_trP1[t]])

    def emit_proj_fm(tc):
        qkb = b(5)
        vb = b(6)
        mmw = [ev_trdma[tc], ev_wqkv]
        if tc >= 1:
            mmw.append(ev_vc1_d[tc - 1])
        first = True
        for col0, wd, dstp in ((0, 256, qkb[:, 0:256]),
                               (256, 256, qkb[:, 256:512])):
            for dc in range(DC):
                ev_qkmm[tc] = p.op("tensor", lambda te, dc=dc, c0=col0, wd=wd,
                                   dp=dstp, tc=tc: te.matmul(
                    dp, lhsT=lnR[:, dc * 2048 + tc * 128: dc * 2048 + (tc + 1) * 128],
                    rhs=wqkv[:, dc * 1536 + c0: dc * 1536 + c0 + wd],
                    start=(dc == 0), stop=(dc == DC - 1)),
                    waits=mmw if first else (),
                    inc="t" if (dc == DC - 1 and col0 == 256) else None)
                first = False
        vdst = v_tm[:, tc * H * VW:(tc + 1) * H * VW].rearrange(
            "p (h w) -> p h w", w=VW)
        wv0 = [ev_vc0_d[tc - 1]] if tc >= 1 else [ev_wqkv_v]
        ev_vmm0 = None
        for dc in range(DC):
            wvd = wv0 if dc == 0 else (
                [("ldwv2", 16)] if (dc == 4 and tc == 0) else ())
            ev_vmm0 = p.op("tensor", lambda te, dc=dc, tc=tc: te.matmul(
                vb, lhsT=lnR[:, dc * 2048 + tc * 128: dc * 2048 + (tc + 1) * 128],
                rhs=wqkv[:, dc * 1536 + 512: dc * 1536 + 1024],
                start=(dc == 0), stop=(dc == DC - 1)),
                waits=wvd,
                inc="t" if dc == DC - 1 else None)

        qk = f32s[:, (tc % 3) * 512: (tc % 3) * 512 + 512]
        evw = [ev_qkmm[tc]]
        if tc >= 3:
            evw.append(ev_fm_ap[tc - 3])
        ev_qkevac[tc] = p.op("scalar", lambda sc, qk=qk, qkb=qkb: sc.copy(
            qk, qkb), waits=evw)

        ev_vmm1 = None
        for dc in range(DC):
            ev_vmm1 = p.op("tensor", lambda te, dc=dc, tc=tc: te.matmul(
                qkb, lhsT=lnR[:, dc * 2048 + tc * 128: dc * 2048 + (tc + 1) * 128],
                rhs=wqkv[:, dc * 1536 + 1024: dc * 1536 + 1536],
                start=(dc == 0), stop=(dc == DC - 1)),
                waits=[ev_qkevac[tc]] if dc == 0 else (),
                inc="t" if dc == DC - 1 else None)
        vs0 = vb.rearrange("p (h w) -> p h w", w=64)
        vs1 = qkb.rearrange("p (h w) -> p h w", w=64)
        ev_vc0_d[tc] = p.op("scalar", lambda sc, d=vdst, s0=vs0: sc.copy(
            d[:, 0:8, 0:64], s0), waits=[ev_vmm0])
        if tc + 2 <= NT - 1 and lntr_next[0] <= tc + 2:
            emit_ln_tr(tc + 2)
            lntr_next[0] += 1
        ev_vc1_d[tc] = p.op("vector", lambda v, d=vdst, s0=vs1: v.tensor_copy(
            d[:, 8:16, 0:64], s0), waits=[ev_vmm1])
        ev_v[tc] = p.op("vector", lambda v, d=vdst: v.memset(d[:, :, 64:65], 1.0),
                        waits=[ev_vc0_d[tc], ev_vc1_d[tc]])
        if tc + 1 <= 8 and tc + 1 not in ev_trdma and tc + 1 in ev_trP1:
            emit_tr_evac(tc + 1)

        qk3 = qk.rearrange("p (a c) -> p a c", c=FD)
        c0, c1 = (tc % 3) * 32, 96 + (tc % 3) * 32
        wred = [ev_qkevac[tc]]
        if tc >= 3:
            wred.append(ev_fm_ap[tc - 3])
        ev_red1 = p.op("vector", lambda v, qk3=qk3, c0=c0: v.tensor_reduce(
            fmst[:, c0:c0 + 32], qk3, AX.X, ALU.add), waits=wred)
        ev_mu = p.op("vector", lambda v, c0=c0: v.tensor_scalar_mul(
            fmst[:, c0:c0 + 32], fmst[:, c0:c0 + 32], 1.0 / FD),
            waits=[ev_red1])
        mu_b = fmst[:, c0:c0 + 32][:, :, None].broadcast_to([128, 32, FD])
        rs_b = fmst[:, c1:c1 + 32][:, :, None].broadcast_to([128, 32, FD])
        apw = [ev_mu]
        if tc == 0:
            apw += [ev_gm, ev_gb]
        ev_a = p.op("vector", lambda v, qk3=qk3, mu_b=mu_b: v.tensor_tensor(
            qk3, qk3, mu_b, ALU.subtract), waits=apw)
        sqs = sqj[:, 1024 + (tc % 3) * 512:1536 + (tc % 3) * 512]
        fw2 = [ev_a]
        if tc >= 3:
            fw2.append(ev_s2red[tc - 3])
        ev_fsq = p.op("scalar", lambda sc, qk=qk, sqs=sqs: sc.activation(
            sqs, qk, AF.Square), waits=fw2)
        sq3 = sqs.rearrange("p (a c) -> p a c", c=FD)
        ev_s2red[tc] = p.op("vector", lambda v, sq3=sq3, c1=c1: v.tensor_reduce(
            fmst[:, c1:c1 + 32], sq3, AX.X, ALU.add), waits=[ev_fsq])
        ev_va = p.op("vector", lambda v, c1=c1: v.tensor_scalar(
            fmst[:, c1:c1 + 32], fmst[:, c1:c1 + 32], 1.0 / FD, EPS,
            ALU.mult, ALU.add), waits=[ev_s2red[tc]])
        ev_frc = p.op("scalar", lambda sc, c1=c1: sc.activation(
            fmst[:, c1:c1 + 32], fmst[:, c1:c1 + 32], AF.Ln), waits=[ev_va])
        ev_fsqr = p.op("scalar", lambda sc, c1=c1: sc.activation(
            fmst[:, c1:c1 + 32], fmst[:, c1:c1 + 32], AF.Exp, scale=-0.5),
            waits=[ev_frc])
        ev_b = p.op("vector", lambda v, qk3=qk3, rs_b=rs_b: v.tensor_tensor(
            qk3, qk3, rs_b, ALU.mult), waits=[ev_fsqr])
        ev_c = p.op("vector", lambda v, qk=qk: v.tensor_tensor(
            qk, qk, gmh, ALU.mult), waits=[ev_b])
        ev_fm_ap[tc] = ev_xn[tc] = p.op(
            "vector", lambda v, qk=qk, tc=tc: v.tensor_tensor(
                xn_qk[:, tc * 512:(tc + 1) * 512], qk, bth, ALU.add),
            waits=[ev_c])

    ev_vc0_d = {}
    ev_vc1_d = {}
    lntr_next = [0]

    def emit_tile(t):
        while lntr_next[0] <= min(t + 1, NT - 1):
            emit_ln_tr(lntr_next[0])
            lntr_next[0] += 1
        if t not in ev_trdma:
            emit_tr_evac(t)
        emit_proj_fm(t)

    for t in range(8):
        emit_tile(t)
    for t in range(8, NT):
        if t not in ev_trdma:
            emit_tr_evac(t)
        while lntr_next[0] <= min(t + 2, NT - 1):
            emit_ln_tr(lntr_next[0])
            lntr_next[0] += 1
    ev_p2_t = None

    if stop == "p2":
        for t in range(8, NT):
            emit_tile(t)
        dbg("dbg_ln1fm", lnR,
            [("tl0", p.cnt["tl0"]), ("tl1", p.cnt["tl1"])])
        dbg("dbg_vtm", v_tm, [ev_v[NT - 1]])
        p.op("gpsimd", lambda g: g.nop(),
             waits=[("st", p.cnt["st"]), ("std", p.cnt["std"])])
        return _finish(nc, p)

    ev_kr = {}
    ev_trq = {}
    ev_trk = {}
    ev_evq = {}
    ev_evk = {}
    ev_pe_done = {}
    ev_mask_all = {}
    ev_omm_all = {}
    ev_oev_all = {}
    ev_stmm_all = {}
    ev_stev_all = {}
    ev_pair_tr = {}
    ev_pair_ev = {}
    ev_at2_all = {}

    worder = [(pr, sq, hp) for pr in range(8) for sq in range(4)
              for hp in range(2)]

    def whs(w):
        pr, sq, hp = worder[w]
        return 2 * pr + hp, sq, hp

    xn3 = xn_qk.rearrange("p (g x) -> p g x", x=512)

    def kr_ops(w):
        h, sq, hp = whs(w)
        pg = (w % 4) * 1024
        gc0 = sq * 4
        evs = []
        for koff, dof, eng, sm in ((256, 1024, "gpsimd", "g"),
                                   (0, 0, "gpsimd", "g")):
            w_ = [ev_xn[gc0 + 3]]
            if w >= 4:
                w_.append(ev_pe_done[w - 4])
                w_.append(ev_evq.get(w - 4))
                w_.append(ev_evk.get(w - 4))
            xs3 = xn3[:, gc0:gc0 + 4, koff + h * FD: koff + (h + 1) * FD]
            o4 = qkf_tm[:, 2 * pg + dof: 2 * pg + dof + 1024].rearrange(
                "p (c a b) -> p c a b", a=FD, b=FD)
            i0 = xs3[:, :, :, None].broadcast_to([128, 4, FD, FD])
            i1 = xs3[:, :, None, :].broadcast_to([128, 4, FD, FD])
            evs.append(p.op(eng, lambda en, o4=o4, i0=i0, i1=i1:
                            en.tensor_tensor(o4, i0, i1, ALU.mult),
                            waits=w_, inc=sm))
        return evs

    def dma_tq(w, extra=()):
        h, sq, hp = whs(w)
        pgt = (w % 4) * 2048
        pgf = (w % 4) * 2048
        w_ = [ev_kr[w][1], ev_kr[w][0]] + list(extra)
        if w >= 4:
            w_.append(ev_pe_done[w - 4])
        dst = qkf_fm[:, pgf:pgf + 2048].rearrange("p (m q) -> p m q", q=128)
        return p.op("sync", lambda sy, dst=dst, pgt=pgt: sy.dma_start(
            out=dst, in_=qkf_tm[:, pgt:pgt + 2048], transpose=True),
            waits=w_, inc=f"tq{w % 4}", amt=16)

    nu_stop = NU
    if stop and str(stop).startswith("attn") and len(str(stop)) > 4:
        nu_stop = int(str(stop)[4:])

    ev_kr[0] = kr_ops(0)
    ev_kr[1] = kr_ops(1)
    ev_evq[0] = ev_evk[0] = dma_tq(0, extra=ld_gate)
    ev_evq[1] = ev_evk[1] = dma_tq(1)

    def mkctx(w):
        h, sq, hp = whs(w)
        return dict(w=w, h=h, sq=sq, hp=hp, pr=h // 2, pp=(h // 2) % 2,
                    pg=(w % 4) * 1024, pgt=(w % 4) * 1024,
                    sb=0 if hp == 0 else 7, ob=1 + hp,
                    ab=3 + hp, stoff=hp * 260)

    def do_attT(cx, c):
        h, sq, hp, pg, ab = cx["h"], cx["sq"], cx["hp"], cx["pg"], cx["ab"]
        gc = sq * 4 + c
        w_ = [ev_evq[cx["w"]], ev_evk[cx["w"]]] if c == 0 else []
        if cx["w"] < 2 and c == 0:
            w_.append(ev_trdma[14 + cx["hp"]])
        keym = (h, gc - 1) if gc >= 1 else (h - 2, 15)
        if keym in ev_mask_all:
            w_.append(ev_mask_all[keym])
        if keym in ev_at2_all:
            w_.append(ev_at2_all[keym])
        ev = None
        for fc in range(2):
            ev = p.op("tensor", lambda te, c=c, fc=fc, ab=ab, pg=pg:
                      te.matmul(
                b(ab, 128),
                lhsT=qkf_fm[:, 2 * pg + 1024 + (c * 2 + fc) * 128:
                            2 * pg + 1024 + (c * 2 + fc + 1) * 128],
                rhs=qkf_fm[:, 2 * pg + (c * 2 + fc) * 128:
                           2 * pg + (c * 2 + fc + 1) * 128],
                start=(fc == 0), stop=(fc == 1 and c == 0)),
                waits=w_ if fc == 0 else (),
                inc="t" if fc == 1 else None)
        if c >= 1:
            for fc in range(2):
                ev = p.op("tensor", lambda te, c=c, fc=fc, ab=ab, pg=pg:
                          te.matmul(
                    b(ab)[:, 128:256],
                    lhsT=qkf_fm[:, 2 * pg + 1024 + ((c - 1) * 2 + fc) * 128:
                                2 * pg + 1024 + ((c - 1) * 2 + fc + 1) * 128],
                    rhs=qkf_fm[:, 2 * pg + (c * 2 + fc) * 128:
                               2 * pg + (c * 2 + fc + 1) * 128],
                    start=False, stop=(fc == 1)),
                    inc="t" if fc == 1 else None)
        return ev

    def do_mask(cx, c, ev_attT):
        h, sq, hp, ab = cx["h"], cx["sq"], cx["hp"], cx["ab"]
        gc = sq * 4 + c
        dstm = at_sb[:, hp * 256: hp * 256 + 128]
        w_ = [ev_attT]
        if cx["w"] == 0 and c == 0:
            w_.append(ev_consts)
        keyo = (h, gc - 1) if gc >= 1 else (h - 2, 15)
        if keyo in ev_omm_all:
            w_.append(ev_omm_all[keyo])
        ev_mask_all[(h, gc)] = p.op(
            "vector", lambda v, dstm=dstm, ab=ab:
            v.tensor_tensor(dstm, b(ab, 128), maskc, ALU.mult),
            waits=w_, inc="v")
        if c >= 1:
            dst2 = at_sb[:, hp * 256 + 128: hp * 256 + 256]
            ev_at2_all[(h, gc)] = p.op(
                "scalar", lambda sc, dst2=dst2, ab=ab:
                sc.copy(dst2, b(ab)[:, 128:256]), waits=[ev_attT], inc="s")

    def do_oz(cx, c):
        h, sq, hp, pg = cx["h"], cx["sq"], cx["hp"], cx["pg"]
        pgt = cx["pgt"]
        sb, ob, stoff, pr, pp = (cx["sb"], cx["ob"], cx["stoff"], cx["pr"],
                                 cx["pp"])
        gc = sq * 4 + c
        a_ap = at_sb[:, hp * 256: hp * 256 + 128]
        vsl = v_tm[:, gc * H * VW + h * VW: gc * H * VW + (h + 1) * VW]
        w_ = [ev_mask_all[(h, gc)], ev_v[gc]]
        use_mid = (c >= 1)
        if use_mid:
            w_.append(ev_at2_all[(h, gc)])
            if gc >= 2:
                w_.append(ev_stev_all[(h, gc - 2)])
            if gc >= 1:
                w_.append(ev_v[gc - 1])
        elif gc > 0:
            w_.append(ev_stev_all[(h, gc - 1)])
        keyo = (h, gc - 1) if gc >= 1 else (h - 2, 15)
        if keyo in ev_oev_all:
            w_.append(ev_oev_all[keyo])
        n_inter = 0 if (gc == 0 or (use_mid and gc < 2)) else 2
        n_tot = 1 + (1 if use_mid else 0) + n_inter
        k = 1
        ev_omm = p.op("tensor",
                      lambda te, a_ap=a_ap, vsl=vsl, ob=ob, single=(n_tot == 1):
                      te.matmul(b(ob, VW), lhsT=a_ap, rhs=vsl, start=True,
                                stop=single), waits=w_,
                      inc="t" if n_tot == 1 else None)
        if use_mid:
            k += 1
            a2 = at_sb[:, hp * 256 + 128: hp * 256 + 256]
            v2 = v_tm[:, (gc - 1) * H * VW + h * VW:
                      (gc - 1) * H * VW + (h + 1) * VW]
            ev_omm = p.op("tensor",
                          lambda te, a2=a2, v2=v2, ob=ob, last=(k == n_tot):
                          te.matmul(b(ob, VW), lhsT=a2, rhs=v2, start=False,
                                    stop=last),
                          inc="t" if k == n_tot else None)
        if n_inter:
            soff = stoff + ((gc % 2) if use_mid else ((gc - 1) % 2)) * 130
            for fc in range(2):
                k += 1
                ev_omm = p.op("tensor",
                              lambda te, c=c, fc=fc, ob=ob, pg=pg,
                              soff=soff, last=(k == n_tot): te.matmul(
                    b(ob, VW),
                    lhsT=qkf_fm[:, 2 * pg + (c * 2 + fc) * 128:
                                2 * pg + (c * 2 + fc + 1) * 128],
                    rhs=st_bf[:, soff + fc * VW: soff + (fc + 1) * VW],
                    start=False, stop=last),
                    inc="t" if k == n_tot else None)
        ev_omm_all[(h, gc)] = ev_omm
        if gc < 15:
            wst = [ev_v[gc]]
            if c == 0:
                wst.append(ev_kr[cx["w"]][0])
            if gc > 0:
                wst.append(ev_stev_all[(h, gc - 1)])
            else:
                if (h - 2, 14) in ev_stev_all:
                    wst.append(ev_stev_all[(h - 2, 14)])
                if hp == 1 and (pr - 1) in ev_pair_ev:
                    wst.append(ev_pair_ev[pr - 1])
            ev_stmm = None
            for fc in range(2):
                ev_stmm = p.op("tensor",
                               lambda te, c=c, fc=fc, pg=pgt, gc=gc, h=h,
                               sb=sb: te.matmul(
                    b(sb)[:, fc * 128: fc * 128 + VW],
                    lhsT=qkf_tm[:, 2 * pg + 1024 + c * 256 + fc * 128:
                                2 * pg + 1024 + c * 256 + fc * 128 + 128],
                    rhs=v_tm[:, gc * H * VW + h * VW:
                             gc * H * VW + (h + 1) * VW],
                    start=(gc == 0 and fc == 0), stop=(gc == 14),
                    skip_group_check=True),
                    waits=wst if fc == 0 else (),
                    inc="t" if fc == 1 else None)
            ev_stmm_all[(h, gc)] = ev_stmm
            src2 = b(sb, 256).rearrange("p (f q) -> p f q", f=2)[:, :, 0:VW]
            so2 = stoff + (gc % 2) * 130
            dst2 = st_bf[:, so2:so2 + 130].rearrange(
                "p (f q) -> p f q", q=VW)
            ev_stev_all[(h, gc)] = p.op(
                "scalar", lambda sc, s0=src2, d0=dst2: sc.copy(d0, s0),
                waits=[ev_stmm, ev_omm_all[(h, gc)]], inc="s")
        zslot = hp * 2 + gc % 2
        zraw = zr[:, zslot:zslot + 1]
        zrec = zr[:, 4 + zslot: 5 + zslot]
        wz = [ev_omm_all[(h, gc)]]
        keyz = (h, gc - 2) if gc >= 2 else (h - 2, gc + 14)
        if keyz in ev_oev_all:
            wz.append(ev_oev_all[keyz])
        ev_z0 = p.op("vector", lambda v, zraw=zraw, ob=ob:
                     v.tensor_scalar_add(
            zraw, b(ob, VW)[:, 64:65], EPS_Z), waits=wz)
        ev_zr = p.op("vector", lambda v, zrec=zrec, zraw=zraw:
                     v.reciprocal(zrec, zraw), waits=[ev_z0])
        dsto = opair[:, pp * 2048 + gc * 128 + hp * 64:
                     pp * 2048 + gc * 128 + hp * 64 + 64]
        wo_ = [ev_zr]
        if hp == 0 and gc == 0 and (pr - 2) in ev_pair_tr:
            wo_.append(ev_pair_tr[pr - 2])
        ev_oev_all[(h, gc)] = p.op(
            "scalar", lambda sc, dsto=dsto, ob=ob, zrec=zrec:
            sc.activation(dsto, b(ob, VW)[:, 0:64], AF.Identity,
                          scale=zrec),
            waits=wo_, inc="s")

    nv_stop = (nu_stop + 1) // 2
    for v in range(nv_stop):
        if v == 1:
            for t in range(8, 12):
                emit_tile(t)
        elif v == 2:
            for t in range(12, 16):
                emit_tile(t)
            ev_p2_t = ("t", p.cnt["t"])
        wA, wB = 2 * v, 2 * v + 1
        A, B = mkctx(wA), mkctx(wB)
        pr, sq = worder[wA][0], worder[wA][1]

        e = do_attT(A, 0); do_mask(A, 0, e)
        e = do_attT(B, 0); do_mask(B, 0, e)
        if wA + 2 < NU:
            ev_kr[wA + 2] = kr_ops(wA + 2)
            ev_evq[wA + 2] = ev_evk[wA + 2] = dma_tq(wA + 2)
        do_oz(A, 0)
        e = do_attT(A, 1); do_mask(A, 1, e)
        do_oz(B, 0)
        e = do_attT(B, 1); do_mask(B, 1, e)
        do_oz(A, 1)
        e = do_attT(A, 2); do_mask(A, 2, e)
        if wB + 2 < NU:
            ev_kr[wB + 2] = kr_ops(wB + 2)
        do_oz(B, 1)
        e = do_attT(B, 2); do_mask(B, 2, e)
        do_oz(A, 2)
        e = do_attT(A, 3); do_mask(A, 3, e)
        do_oz(B, 2)
        e = do_attT(B, 3); do_mask(B, 3, e)
        do_oz(A, 3)
        ev_pe_done[wA] = (ev_omm_all[(A["h"], 15)] if sq == 3
                          else ev_stmm_all[(A["h"], sq * 4 + 3)])
        if wB + 2 < NU:
            ev_evq[wB + 2] = ev_evk[wB + 2] = dma_tq(wB + 2)
        do_oz(B, 3)
        ev_pe_done[wB] = (ev_omm_all[(B["h"], 15)] if sq == 3
                          else ev_stmm_all[(B["h"], sq * 4 + 3)])

        if sq == 3:
            h = B["h"]
            pp = B["pp"]
            w_ = [ev_oev_all[(h, 15)]]
            if (pr - 2) in ev_pair_ev:
                w_.append(ev_pair_ev[pr - 2])
            if pr == 0:
                w_.append(ev_p2_t)
            dst = lnR[:, pr * 2048:(pr + 1) * 2048].rearrange(
                "p (m q) -> p m q", q=128)
            if pr < 7:
                ev_pair_ev[pr] = p.op("sync", lambda sy, dst=dst, pp=pp:
                                      sy.dma_start(
                    out=dst, in_=opair[:, pp * 2048:(pp + 1) * 2048],
                    transpose=True), waits=w_, inc="tp", amt=16)
                ev_pair_tr[pr] = ev_pair_ev[pr]
            else:
                w_.append(ev_oev_all[(h - 1, 15)])
                ev_t7 = None
                for blk in range(16):
                    tb7 = 1 + blk // 8
                    ev_t7 = p.op("tensor", lambda te, blk=blk, tb7=tb7,
                                 pp=pp: te.transpose(
                        bb(tb7)[:, (blk % 8) * 128:(blk % 8 + 1) * 128],
                        in_=opair[:, pp * 2048 + blk * 128:
                                  pp * 2048 + (blk + 1) * 128],
                        identity=identc),
                        waits=w_ if blk == 0 else (),
                        inc="t" if blk == 15 else None)
                ev_e7a = p.op("vector", lambda v, dst=dst: v.tensor_copy(
                    dst[:, 0:8, :],
                    bb(1).rearrange("p (m q) -> p m q", q=128)),
                    waits=[ev_t7], inc="v")
                ev_pair_ev[pr] = p.op("vector", lambda v, dst=dst:
                                      v.tensor_copy(
                    dst[:, 8:16, :],
                    bb(2).rearrange("p (m q) -> p m q", q=128)),
                    waits=[ev_t7], inc="v")
                ev_pair_tr[pr] = ev_pair_ev[pr]

    w_ld = [ev_p2_t, ("tp", p.cnt["tp"]), ("tq0", p.cnt["tq0"]),
            ("tq1", p.cnt["tq1"]), ("tq2", p.cnt["tq2"]),
            ("tq3", p.cnt["tq3"])]
    wo8 = wo.rearrange("p (c j) -> p c j", c=8)
    d_wo8 = d_wo.rearrange("(c p) j -> p c j", p=128)
    ldos = ["ldo", "ldo2", "ldo3", "ldo4"]
    for h in range(4):
        p.op("gpsimd", lambda g, h=h: g.dma_start(
            out=wo8[:, h * 2:(h + 1) * 2], in_=d_wo8[:, h * 2:(h + 1) * 2]),
            waits=w_ld if h == 0 else (), inc=ldos[h], amt=16)
    ev_gw = p.op("gpsimd", lambda g: g.dma_start(
        out=gw.rearrange("p (c e) -> p c e", c=8),
        in_=d_gw.rearrange("(c p) e -> p c e", p=128)),
        inc="ldg", amt=16)
    for idx in range(2):
        ev_ew[idx] = p.op("gpsimd", lambda g, idx=idx: g.dma_start(
            out=ewb[:, (idx % 2) * 1024:(idx % 2 + 1) * 1024], in_=d_ew[idx]),
            inc=f"ldew{idx % 2}", amt=16)

    if ev_pair_ev:
        dbg("dbg_ofm", lnR, [ev_pair_ev[max(ev_pair_ev)]])
        dbg("dbg_opair", opair, [ev_pair_ev[max(ev_pair_ev)]])
    if stop and str(stop).startswith("attn"):
        p.op("gpsimd", lambda g: g.nop(),
             waits=[("st", p.cnt["st"]), ("std", p.cnt["std"])])
        return _finish(nc, p)

    gate5 = p.counts_event("v", "s", "g") + [("tp", p.cnt["tp"])]
    ev_wo_mm = {}
    ev_ln2 = {}
    ev_ln2ap = {}
    ev_sq3 = {}
    ev_r3 = {}
    ev_r2 = {}
    ev_q2 = {}
    ev_ap2 = {}
    tail2 = {}
    wo_tm = v_tm
    s1c = stats[:, 64:80]
    s2c = stats[:, 80:96]
    rstd2 = stats[:, 96:112]
    nmr2 = stats[:, 112:128]

    def p5_mm(tc):
        base = 0 if tc % 2 == 0 else 2
        pv = psum[:, base * 512: base * 512 + 1024]
        w = gate5 + [("ldo", 16)] + ([ev_ln2ap[tc - 2]] if tc >= 2 else [])
        first = True
        for jh in range(2):
            for mc in range(8):
                wmm = w if first else (
                    [({2: "ldo2", 4: "ldo3", 6: "ldo4"}[mc], 16)]
                    if (tc == 0 and jh == 0 and mc in (2, 4, 6))
                    else ())
                ev_wo_mm[tc] = p.op("tensor", lambda te, mc=mc, jh=jh, tc=tc,
                                    base=base: te.matmul(
                    b(base + jh),
                    lhsT=lnR[:, mc * 2048 + tc * 128: mc * 2048 + (tc + 1) * 128],
                    rhs=wo[:, mc * 1024 + jh * 512: mc * 1024 + (jh + 1) * 512],
                    start=(mc == 0), stop=(mc == 7)),
                    waits=wmm,
                    inc="t" if (mc == 7 and jh == 1) else None)
                first = False
        slot = wo_tm[:, tc * 1024:(tc + 1) * 1024]
        ew_ = [ev_wo_mm[tc]]
        if tc < 2:
            ew_.append(ev_pe_done[NU - 1])
        if tc % 2 == 0:
            ev_ln2ap[tc] = p.op("scalar", lambda sc, slot=slot, pv=pv:
                                sc.copy(slot, pv), waits=ew_)
        else:
            ev_ln2ap[tc] = p.op("vector", lambda v, slot=slot, pv=pv:
                                v.tensor_copy(slot, pv), waits=ew_)
        ev_r2[tc] = p.op("vector", lambda v, tc=tc, slot=slot: v.tensor_reduce(
            s1c[:, tc:tc + 1], slot, AX.X, ALU.add), waits=[ev_ln2ap[tc]])
        qw = [ev_ln2ap[tc]]
        if tc < 2:
            qw += [ev_fm_ap[15], ev_fm_ap[13]]
        ev_q2[tc] = p.op("scalar", lambda sc, tc=tc, slot=slot: sc.activation(
            f32s.bitcast(BF16)[:, (tc % 2) * 1024:(tc % 2) * 1024 + 1024],
            slot, AF.Square, accum_out=s2c[:, tc:tc + 1]), waits=qw)

    def p5_tail(h):
        c0 = h * 8
        s1h = s1c[:, c0:c0 + 8]
        s2h = s2c[:, c0:c0 + 8]
        rsh = rstd2[:, c0:c0 + 8]
        nmh = nmr2[:, c0:c0 + 8]
        ev_m = p.op("vector", lambda v, s1h=s1h: v.tensor_scalar_mul(
            s1h, s1h, 1.0 / D), waits=[ev_r2[c0 + 7]])
        ev_va = p.op("vector", lambda v, rsh=rsh, s2h=s2h: v.tensor_scalar(
            rsh, s2h, 1.0 / D, EPS, ALU.mult, ALU.add), waits=[ev_q2[c0 + 7]])
        ev_n0 = p.op("vector", lambda v, nmh=nmh, s1h=s1h:
                     v.scalar_tensor_tensor(
            nmh, s1h, -1.0, s1h, ALU.mult, ALU.mult), waits=[ev_m])
        ev_ad = p.op("vector", lambda v, rsh=rsh, nmh=nmh: v.tensor_tensor(
            rsh, rsh, nmh, ALU.add), waits=[ev_va, ev_n0])
        ev_rc = p.op("scalar", lambda sc, rsh=rsh: sc.activation(
            rsh, rsh, AF.Ln), waits=[ev_ad])
        ev_sq = p.op("scalar", lambda sc, rsh=rsh: sc.activation(
            rsh, rsh, AF.Exp, scale=-0.5), waits=[ev_rc])
        tail2[h] = p.op("vector", lambda v, nmh=nmh, s1h=s1h, rsh=rsh:
                        v.scalar_tensor_tensor(nmh, s1h, -1.0, rsh,
                                               ALU.mult, ALU.mult),
                        waits=[ev_sq, ev_n0])

    def p5_b(tc):
        slot = wo_tm[:, tc * 1024:(tc + 1) * 1024]
        ls4 = tc % 4
        if ls4 < 2:
            lslot = lnt[:, ls4 * 1024:(ls4 + 1) * 1024]
        else:
            lslot = hsb[:, (ls4 - 2) * 1024:(ls4 - 1) * 1024]
        aw = [tail2[tc // 8]]
        if tc < 2:
            aw.append(ev_trP1[14 + tc])
        if tc >= 4:
            aw.append(ev_ln2[tc - 4])
        ev_ap2[tc] = p.op("scalar", lambda sc, slot=slot, lslot=lslot, tc=tc:
                          sc.activation(
            lslot, slot, AF.Identity, scale=rstd2[:, tc:tc + 1],
            bias=nmr2[:, tc:tc + 1]), waits=aw)
        ev_ln2[tc] = p.op("vector", lambda v, tc=tc, lslot=lslot:
                          v.tensor_tensor(
            x_tm[:, tc * D:(tc + 1) * D], x_tm[:, tc * D:(tc + 1) * D],
            lslot, ALU.add), waits=[ev_ap2[tc]], inc="v")

    def p5_c(tc):
        sqw = [ev_ln2[tc]]
        if tc < 2:
            sqw.append(("tp", p.cnt["tp"]))
        else:
            sqw.append(ev_sq3[tc - 2])
        ev_sq3[tc] = p.op("scalar", lambda sc, tc=tc: sc.activation(
            opair[:, (tc % 2) * 1024:(tc % 2) * 1024 + 1024],
            x_tm[:, tc * D:(tc + 1) * D], AF.Square,
            accum_out=stats[:, 144 + tc:145 + tc]), waits=sqw)
        if tc % 4 == 3:
            g = tc // 4
            src3g = x_tm[:, g * 4096:(g + 1) * 4096].rearrange(
                "p (t d) -> p t d", d=D)
            ev_r3[g] = p.op("vector", lambda v, g=g, src3g=src3g:
                            v.tensor_reduce(
                stats[:, 128 + g * 4: 132 + g * 4], src3g, AX.X, ALU.add),
                waits=[ev_ln2[tc]])

    for tc in range(8):
        p5_mm(tc)
    p5_tail(0)
    for tc in range(4):
        p5_mm(tc + 8)
        p5_b(2 * tc)
        p5_b(2 * tc + 1)
    for tc in range(4):
        p5_mm(tc + 12)
        p5_c(2 * tc)
        p5_c(2 * tc + 1)


    ev_ln3fm = {}
    ev_tr3 = {}
    s1b = stats[:, 128:144]
    s2b = stats[:, 144:160]
    rstd3 = stats[:, 160:176]
    nmr3 = stats[:, 176:192]
    tail3 = {}

    def ln3_tail(h):
        c0 = h * 8
        s1h = s1b[:, c0:c0 + 8]
        s2h = s2b[:, c0:c0 + 8]
        rsh = rstd3[:, c0:c0 + 8]
        nmh = nmr3[:, c0:c0 + 8]
        ev_m3 = p.op("vector", lambda v, s1h=s1h: v.tensor_scalar_mul(
            s1h, s1h, 1.0 / D), waits=[ev_r3[2 * h + 1]])
        ev_va3 = p.op("vector", lambda v, rsh=rsh, s2h=s2h: v.tensor_scalar(
            rsh, s2h, 1.0 / D, EPS, ALU.mult, ALU.add),
            waits=[ev_sq3[c0 + 7]])
        ev_n03 = p.op("vector", lambda v, nmh=nmh, s1h=s1h:
                      v.scalar_tensor_tensor(
            nmh, s1h, -1.0, s1h, ALU.mult, ALU.mult), waits=[ev_m3])
        ev_ad3 = p.op("vector", lambda v, rsh=rsh, nmh=nmh: v.tensor_tensor(
            rsh, rsh, nmh, ALU.add), waits=[ev_va3, ev_n03])
        ev_rc3 = p.op("scalar", lambda sc, rsh=rsh: sc.activation(
            rsh, rsh, AF.Ln), waits=[ev_ad3])
        ev_sq3e = p.op("scalar", lambda sc, rsh=rsh: sc.activation(
            rsh, rsh, AF.Exp, scale=-0.5), waits=[ev_rc3])
        tail3[h] = p.op("vector", lambda v, nmh=nmh, s1h=s1h, rsh=rsh:
                        v.scalar_tensor_tensor(nmh, s1h, -1.0, rsh,
                                               ALU.mult, ALU.mult),
                        waits=[ev_sq3e, ev_n03])

    def emit_p6a(t):
        xt = x_tm[:, t * D:(t + 1) * D]
        ls = 2048 + (t % 2) * 1024
        lslot = opair[:, ls:ls + 1024]
        w = [tail3[t // 8]]
        if t < 2:
            w.append(("tp", p.cnt["tp"]))
        if t >= 2:
            w.append(ev_tr3[t - 2])
        ev_ap = p.op("scalar", lambda sc, o=lslot, sa=xt, t=t: sc.activation(
            o, sa, AF.Identity, scale=rstd3[:, t:t + 1],
            bias=nmr3[:, t:t + 1]), waits=w)
        tb = 6 + (t % 2)
        trw = [ev_ap, ev_wo_mm[NT - 1]]
        if t >= 2:
            trw.append(ev_ln3fm[t - 2])
        for dc in range(DC):
            ev_tr3[t] = p.op("tensor", lambda te, dc=dc, ls=ls, tb=tb:
                             te.transpose(
                bb(tb)[:, dc * 128:(dc + 1) * 128],
                in_=opair[:, ls + dc * 128: ls + (dc + 1) * 128],
                identity=identc),
                waits=trw if dc == 0 else (), inc="t" if dc == DC - 1 else None)
        dstv = lnR.rearrange("p (c n) -> p c n", c=8)[:, :, t * 128:(t + 1) * 128]
        srcv = bb(tb).rearrange("p (c n) -> p c n", c=8)
        ev_ln3fm[t] = p.op("vector", lambda v, d=dstv, s0=srcv: v.tensor_copy(
            d, s0), waits=[ev_tr3[t]], inc="v")

    def p7_mms(idx, tts, w):
        hc, e = idx // E, idx % E
        buf = ewb[:, (idx % 2) * 1024:(idx % 2) * 1024 + 1024]
        bs = 0 if e % 2 == 0 else 4
        for tt in tts:
            wtt = list(w) if tt == tts[0] else []
            if idx < 2:
                wtt.append(ev_ln3fm[4 * tt + 3])
            if idx == 0:
                wtt.append(ev_ln2ap[14 if tt < 2 else 15])
            if idx == 1 and tt >= 2:
                wtt.append(ev_ln3fm[12 + tt])
            for dc in range(DC):
                ev_ffn_pe[idx] = p.op("tensor", lambda te, dc=dc, tt=tt,
                                      bs=bs, buf=buf: te.matmul(
                    b(bs + tt), lhsT=buf[:, dc * 128:(dc + 1) * 128],
                    rhs=lnR[:, dc * 2048 + tt * 512: dc * 2048 + (tt + 1) * 512],
                    start=(dc == 0), stop=(dc == DC - 1)),
                    waits=wtt if dc == 0 else (),
                    inc="t" if (dc == DC - 1 and tt == 3) else None)

    ln3_tail(0)
    for t in range(8):
        emit_p6a(t)
    ev_ffn_pe = {}
    p7_mms(0, [0, 1], [ev_ew[0]])
    p5_tail(1)
    for tc in range(8, 16):
        p5_b(tc)
    for tc in range(8, 16):
        p5_c(tc)
    ln3_tail(1)
    for t in range(8, NT):
        emit_p6a(t)

    dbg("dbg_x1", x_tm, [ev_ln2[NT - 1]])
    dbg("dbg_ln3fm", lnR, [ev_ln3fm[NT - 1]])
    if stop == "p5":
        p.op("gpsimd", lambda g: g.nop(),
             waits=[("st", p.cnt["st"]), ("std", p.cnt["std"])])
        return _finish(nc, p)

    dbg("dbg_ln3fm", lnR, [ev_ln3fm[NT - 1]])

    ev_gmm = None
    for tc in range(NT):
        w = [ev_ln3fm[tc]] + ([ev_gw] if tc == 0 else [])
        first = True
        for dc in range(DC):
            ev_gmm = p.op("tensor", lambda te, dc=dc, tc=tc: te.matmul(
                b(4)[:, tc * E:(tc + 1) * E],
                lhsT=lnR[:, dc * 2048 + tc * 128: dc * 2048 + (tc + 1) * 128],
                rhs=gw[:, dc * E:(dc + 1) * E],
                start=(dc == 0), stop=(dc == DC - 1)),
                waits=w if first else (), inc="t" if dc == DC - 1 else None)
            first = False
    gts = fmst[:, 0:128]
    g3 = gts.rearrange("p (t e) -> p t e", e=E)
    ind = hsb[:, 1536:1792].bitcast(F32)
    i3 = ind.rearrange("p (t e) -> p t e", e=E)
    gb_b = gbr[:, None, :].broadcast_to([128, NT, E])
    ev_g0 = p.op("vector", lambda v: v.tensor_tensor(
        g3, b(4)[:, 0:128].rearrange("p (t e) -> p t e", e=E), gb_b, ALU.add),
        waits=[ev_gmm, ev_consts, ev_ln3fm[NT - 1]])
    ev_m1 = p.op("vector", lambda v: v.tensor_reduce(
        gsc[:, 0:16], g3, AX.X, ALU.max), waits=[ev_g0])
    m1_b = gsc[:, 0:16][:, :, None].broadcast_to([128, NT, E])
    ev_i1 = p.op("vector", lambda v: v.tensor_tensor(
        i3, g3, m1_b, ALU.is_ge), waits=[ev_m1])
    ev_st = p.op("vector", lambda v: v.scalar_tensor_tensor(
        i3, i3, -1e30, g3, ALU.mult, ALU.add), waits=[ev_i1])
    ev_m2 = p.op("vector", lambda v: v.tensor_reduce(
        gsc[:, 16:32], i3, AX.X, ALU.max), waits=[ev_st])
    m2_b = gsc[:, 16:32][:, :, None].broadcast_to([128, NT, E])
    ev_i2 = p.op("vector", lambda v: v.tensor_tensor(
        i3, g3, m2_b, ALU.is_ge), waits=[ev_m2])
    cstate = {}

    def emit_counts_head():
        ev_cntmm = p.op("tensor", lambda te: te.matmul(
            b(5)[0:1, 0:128], lhsT=onesc, rhs=ind,
            start=True, stop=True), waits=[ev_i2, ev_consts], inc="t")
        cnt_view = b(5)[0:1, 0:NT * E].rearrange("p (t e) -> p e t", e=E)
        cstate["cl"] = p.op("vector", lambda v: v.tensor_reduce(
            cnt_l, cnt_view, AX.X, ALU.add),
            waits=[ev_cntmm], inc="v")

    def emit_counts_tail():
        p.op("sync", lambda sy: sy.dma_start(out=d_ccin, in_=cnt_l),
             waits=[cstate["cl"]], inc="ccs", amt=16)
        if cc_mode == "ar":
            p.op("gpsimd", lambda g: g.collective_compute(
                "AllReduce", ALU.add, replica_groups=[list(range(8))],
                ins=[d_ccin], outs=[d_ccout]), waits=[("ccs", 16)],
                inc="cca", amt=1)
            p.op("sync", lambda sy: sy.dma_start(
                out=cb, in_=d_ccout[0:1, :].broadcast_to([128, E])),
                waits=[("cca", 1)], inc="ccs", amt=16)
        else:
            p.op("sync", lambda sy: sy.dma_start(
                out=cb, in_=d_ccin[0:1, :].broadcast_to([128, E])),
                waits=[("ccs", 16)], inc="ccs", amt=16)
        ev_cb = p.op("vector", lambda v: v.tensor_scalar_mul(
            cb, cb, TOTAL_INV), waits=[("ccs", 32)])
        ev_ebs = p.op("vector", lambda v: v.tensor_tensor(
            ebs.rearrange("p (h2 e) -> p h2 e", e=E),
            eb.rearrange("p (h2 e) -> p h2 e", e=E),
            cb[:, None, :].broadcast_to([128, 8, E]), ALU.mult),
            waits=[ev_cb, ev_consts], inc="v")
        cstate["cb"] = ev_cb
        cstate["ebs"] = ev_ebs

    if stop == "p6":
        emit_counts_head()
        emit_counts_tail()
        dbg("dbg_cb", cb, [cstate["cb"]])
        p.op("gpsimd", lambda g: g.nop(),
             waits=[("st", p.cnt["st"]), ("std", p.cnt["std"])])
        return _finish(nc, p)

    acc = f32s.bitcast(BF16)[:, 0:2048]
    ffn_fm = v_tm
    ev_ffn_s = {}
    ev_add = {}
    HC = 8
    rbuf = {}
    for e in range(E):
        if e < 4:
            rbuf[e] = qkf_tm[:, e * 2048:(e + 1) * 2048]
        elif e < 6:
            rbuf[e] = qkf_fm[:, (e - 4) * 2048:(e - 4) * 2048 + 2048]
        else:
            rbuf[e] = xn_qk[:, (e - 6) * 2048:(e - 6) * 2048 + 2048]
    gate7vg = p.counts_event("v", "g")
    tq_all = [("tq0", p.cnt["tq0"]), ("tq1", p.cnt["tq1"]),
              ("tq2", p.cnt["tq2"]), ("tq3", p.cnt["tq3"])]
    for hc in range(HC):
        for e in range(E):
            idx = hc * E + e
            w = [ev_ew[idx]]
            if idx >= 2:
                w.append(ev_ffn_s[idx - 2])
            if idx == 1:
                w += [ev_g0, cstate["cl"]]
            bs = 0 if e % 2 == 0 else 4
            if idx == 0:
                p7_mms(0, [2, 3], [])
                emit_counts_head()
            else:
                p7_mms(idx, [0, 1, 2, 3], w)
            if idx + 2 < 64:
                ev_ew[idx + 2] = p.op("gpsimd", lambda g, idx=idx: g.dma_start(
                    out=ewb[:, (idx % 2) * 1024:(idx % 2) * 1024 + 1024],
                    in_=d_ew[idx + 2]), waits=[ev_ffn_pe[idx]],
                    inc=f"ldew{idx % 2}", amt=16)
            if idx == 1:
                emit_counts_tail()
            if hc == 0:
                sw = [ev_ffn_pe[idx]]
                if idx == 0:
                    sw += tq_all + gate7vg
                ev_ffn_s[idx] = p.op("scalar", lambda sc, e=e, bs=bs:
                                     sc.activation(
                    rbuf[e], psum[:, bs * 512: bs * 512 + 2048], AF.Relu,
                    bias=eb[:, e:e + 1]), waits=sw)
                if e == E - 1:
                    cw = [cstate["ebs"], ev_ffn_s[0]]
                    cw += [ev_fm_ap[i] for i in (12, 13, 14, 15)]
                    ev_cmb = p.op("vector", lambda v: v.tensor_scalar_mul(
                        acc, rbuf[0], cb[:, 0:1]), waits=cw)
                    for ce in range(1, E):
                        dst_c = ffn_fm[:, 0:2048] if ce == E - 1 else acc
                        ev_cmb = p.op("vector", lambda v, ce=ce, dst_c=dst_c:
                                      v.scalar_tensor_tensor(
                            dst_c, rbuf[ce], cb[:, ce:ce + 1], acc,
                            ALU.mult, ALU.add),
                            waits=[ev_cmb, ev_ffn_s[ce]])
                    cstate["comb"] = ev_cmb
                continue
            sw = [ev_ffn_pe[idx], cstate["ebs"]]
            if idx == 9:
                sw += [ev_s2red[14], ev_s2red[15]]
            if e == 0:
                sw.append(cstate["comb"] if hc == 1 else ev_add[hc * E - 1])
            if e > 0 and idx >= 2:
                prev_add = ev_add.get(idx - 2)
                if prev_add is not None:
                    sw.append(prev_add)
            if e == 0:
                dsta = acc
            else:
                dsta = hsb[:, (idx % 2) * 2048:(idx % 2) * 2048 + 2048]
            ev_ffn_s[idx] = p.op("scalar", lambda sc, dsta=dsta, bs=bs,
                                 hc=hc, e=e: sc.activation(
                dsta, psum[:, bs * 512: bs * 512 + 2048], AF.Relu,
                scale=cb[:, e:e + 1],
                bias=ebs[:, hc * E + e: hc * E + e + 1]), waits=sw)
            if e > 0:
                dsta = (ffn_fm[:, hc * 2048:(hc + 1) * 2048] if e == E - 1
                        else acc)
                aw = [ev_ffn_s[idx], ev_add.get(idx - 1)]
                if aw[-1] is None:
                    aw = aw[:-1]
                if e == 1 and hc == 1:
                    aw.append(cstate["comb"])
                hsl = hsb[:, (idx % 2) * 2048:(idx % 2) * 2048 + 2048]
                ev_add[idx] = p.op(
                    "vector", lambda ve, dsta=dsta, hsl=hsl:
                    ve.tensor_tensor(dsta, acc, hsl, ALU.add),
                    waits=aw)

    dbg("dbg_ffnfm", v_tm[:, 0:16384], [ev_add[HC * E - 1]])

    gate8 = p.counts_event("v", "s", "g")
    base4 = 3 * 64
    s1a4 = stats[:, base4:base4 + 16]
    s2a4 = stats[:, base4 + 16:base4 + 32]
    rstd4 = stats[:, base4 + 32:base4 + 48]
    nmr4 = stats[:, base4 + 48:base4 + 64]
    ev_tr4 = {}
    ev_r4 = {}
    ev_q4 = {}
    ev_ap4 = {}
    ev_add4 = {}
    tail4 = {}

    def p8_a(t):
        tb = t % 8
        w = list(gate8) if t == 0 else []
        if t >= 8:
            w.append(ev_ap4[t - 8])
        for dc in range(DC):
            ev_tr4[t] = p.op("tensor", lambda te, dc=dc, t=t, tb=tb:
                             te.transpose(
                bb(tb)[:, dc * 128:(dc + 1) * 128],
                in_=ffn_fm[:, dc * 2048 + t * 128: dc * 2048 + (t + 1) * 128],
                identity=identc),
                waits=w if dc == 0 else (), inc="t" if dc == DC - 1 else None)
        rw = [ev_tr4[t]] + (list(gate8) if t == 0 else [])
        ev_r4[t] = p.op("vector", lambda v, t=t, tb=tb: v.tensor_reduce(
            s1a4[:, t:t + 1], bb(tb), AX.X, ALU.add), waits=rw)
        qw = [ev_r4[t]] + list(sq_war()) + (list(gate8) if t == 0 else [])
        ev_q4[t] = p.op("scalar", lambda sc, t=t, tb=tb: sc.activation(
            sqj[:, 0:1024], bb(tb), AF.Square, accum_out=s2a4[:, t:t + 1]),
            waits=qw)
        sq_last[0] = ev_q4[t]

    def p8_tail(h):
        c0 = h * 4
        s1h = s1a4[:, c0:c0 + 4]
        s2h = s2a4[:, c0:c0 + 4]
        rsh = rstd4[:, c0:c0 + 4]
        nmh = nmr4[:, c0:c0 + 4]
        ev_m = p.op("vector", lambda v, s1h=s1h: v.tensor_scalar_mul(
            s1h, s1h, 1.0 / D), waits=[ev_r4[c0 + 3]])
        ev_va = p.op("vector", lambda v, rsh=rsh, s2h=s2h: v.tensor_scalar(
            rsh, s2h, 1.0 / D, EPS, ALU.mult, ALU.add), waits=[ev_q4[c0 + 3]])
        ev_n0 = p.op("vector", lambda v, nmh=nmh, s1h=s1h: v.scalar_tensor_tensor(
            nmh, s1h, -1.0, s1h, ALU.mult, ALU.mult), waits=[ev_m])
        ev_ad = p.op("vector", lambda v, rsh=rsh, nmh=nmh: v.tensor_tensor(
            rsh, rsh, nmh, ALU.add), waits=[ev_va, ev_n0])
        ev_rc = p.op("scalar", lambda sc, rsh=rsh: sc.activation(
            rsh, rsh, AF.Ln), waits=[ev_ad])
        ev_sq = p.op("scalar", lambda sc, rsh=rsh: sc.activation(
            rsh, rsh, AF.Exp, scale=-0.5), waits=[ev_rc])
        tail4[h] = p.op("vector", lambda v, nmh=nmh, s1h=s1h, rsh=rsh:
                        v.scalar_tensor_tensor(nmh, s1h, -1.0, rsh,
                                               ALU.mult, ALU.mult),
                        waits=[ev_sq, ev_n0])

    def p8_b(t):
        tb = t % 8
        tmp = xn_qk[:, tb * 1024:(tb + 1) * 1024]
        aw = [tail4[t // 4]]
        if t == 0:
            aw += gate8
        if t >= 8:
            aw.append(ev_add4[t - 8])
        ev_ap4[t] = p.op("scalar", lambda sc, t=t, tb=tb, tmp=tmp:
                         sc.activation(
            tmp, bb(tb), AF.Identity, scale=rstd4[:, t:t + 1],
            bias=nmr4[:, t:t + 1]), waits=aw)
        xsl = x_tm[:, t * D:(t + 1) * D]
        if t % 2 == 0:
            ev_add4[t] = p.op("vector", lambda v, xsl=xsl, tmp=tmp:
                              v.tensor_tensor(xsl, xsl, tmp, ALU.add),
                              waits=[ev_ap4[t]])
            p.op("gpsimd", lambda g, t=t, xsl=xsl: g.dma_start(
                out=d_out[t * 128:(t + 1) * 128, :], in_=xsl),
                waits=[ev_add4[t]], inc="st", amt=16)
        else:
            fsl = lnR[:, (t // 2) * 2048:(t // 2) * 2048 + 2048].bitcast(F32)
            ev_add4[t] = p.op("vector", lambda v, xsl=xsl, tmp=tmp, fsl=fsl:
                              v.tensor_tensor(fsl, xsl, tmp, ALU.add),
                              waits=[ev_ap4[t]])
            p.op("sync", lambda sy, t=t, fsl=fsl: sy.dma_start(
                out=d_out[t * 128:(t + 1) * 128, :], in_=fsl),
                waits=[ev_add4[t]], inc="st2", amt=16)

    for t in range(8):
        p8_a(t)
    p8_tail(0)
    for t in range(4):
        p8_b(t)
    for t in range(8, 12):
        p8_a(t)
    p8_tail(1)
    for t in range(4, 8):
        p8_b(t)
    for t in range(12, 16):
        p8_a(t)
    p8_tail(2)
    for t in range(8, 12):
        p8_b(t)
    p8_tail(3)
    for t in range(12, 16):
        p8_b(t)

    p.op("gpsimd", lambda g: g.nop(),
         waits=[("st", p.cnt["st"]), ("std", p.cnt["std"]),
                ("st2", p.cnt["st2"])])
    return _finish(nc, p)


def _finish(nc, p):
    from contextlib import ExitStack
    es = ExitStack()
    semh = {name: es.enter_context(nc.semaphore(name)) for name in p.cnt}
    block = es.enter_context(nc.Block())

    @block.sync
    def _(sync):
        _emit(sync, p.ops["sync"], semh)

    @block.tensor
    def _(tensor):
        _emit(tensor, p.ops["tensor"], semh)

    @block.vector
    def _(vector):
        _emit(vector, p.ops["vector"], semh)

    @block.scalar
    def _(scalar):
        _emit(scalar, p.ops["scalar"], semh)

    @block.gpsimd
    def _(gpsimd):
        _emit(gpsimd, p.ops["gpsimd"], semh)

    es.close()
    return nc


def stage_inputs(inputs):
    x_all = np.ascontiguousarray(np.asarray(inputs["L0_attn_hidden_states"]))
    wq = np.asarray(inputs["L0_attn_Wq"])
    wk = np.asarray(inputs["L0_attn_Wk"])
    wv = np.asarray(inputs["L0_attn_Wv"])
    wo_ = np.asarray(inputs["L0_attn_Wo"])
    gamma = np.asarray(inputs["L0_attn_gamma"])
    beta = np.asarray(inputs["L0_attn_beta"])
    gw_ = np.asarray(inputs["L0_ffn_gate_w"])
    gb_ = np.asarray(inputs["L0_ffn_gate_b"])
    ew_ = np.asarray(inputs["L0_ffn_expert_w"])
    eb_ = np.asarray(inputs["L0_ffn_expert_b"])

    wqkv = np.ascontiguousarray(np.concatenate([wq, wk, wv], axis=0).T)
    wo_t = np.ascontiguousarray(wo_.T)
    gw_t = np.ascontiguousarray(gw_.T)
    ew_s = np.ascontiguousarray(
        ew_.reshape(E, 8, 128, 8, 128).transpose(1, 0, 4, 3, 2)
        .reshape(64, 128, 1024))
    eb_s = np.ascontiguousarray(
        eb_.T.reshape(8, 128, E).transpose(1, 0, 2).reshape(128, 64))
    gbr = np.ascontiguousarray(np.broadcast_to(gb_[None, :], (128, E)))
    gmh = np.ascontiguousarray(np.broadcast_to(
        np.tile(gamma, 32)[None, :], (128, 512)).astype(np.float32))
    bth = np.ascontiguousarray(np.broadcast_to(
        np.tile(beta, 32)[None, :], (128, 512)).astype(np.float32))
    maskc = np.triu(np.ones((128, 128), dtype=np.float32)).astype(
        ml_dtypes.bfloat16)
    identc = np.eye(128, dtype=np.float32).astype(ml_dtypes.bfloat16)
    onesc = np.ones((128, 1), dtype=np.float32)
    onesr = np.ones((1, 128), dtype=np.float32)

    shared = dict(wqkv=wqkv, wo=wo_t, gw=gw_t, ew=ew_s, eb=eb_s, gbr=gbr,
                  gmh=gmh, bth=bth, maskc=maskc, identc=identc,
                  onesc=onesc, onesr=onesr)
    in_maps = []
    for bi in range(8):
        m = dict(shared)
        m["x"] = np.ascontiguousarray(x_all[bi])
        in_maps.append(m)
    return in_maps


def _run(inputs, debug=(), stop=None):
    nc = build(debug=debug, stop=stop)
    in_maps = stage_inputs(inputs)
    return run_bass_kernel_spmd(nc, in_maps, core_ids=list(range(8)))


def kernel(**inputs):
    res = _run(inputs, debug=DEBUG, stop=STOP)
    out = np.stack([np.asarray(res.results[i]["out"]) for i in range(8)], axis=0)
    return np.ascontiguousarray(out.astype(np.float32))


# version 51
# speedup vs baseline: 1.0311x; 1.0311x over previous
import numpy as np
import ml_dtypes
import concourse.bass as bass
import concourse.mybir as mybir
from concourse.bass_utils import run_bass_kernel_spmd

F32 = mybir.dt.float32
BF16 = mybir.dt.bfloat16
AF = mybir.ActivationFunctionType
ALU = mybir.AluOpType
AX = mybir.AxisListType

T = 2048
D = 1024
H = 16
FD = 16
HD = 64
VW = 65
E = 8
NT = 16
DC = 8
NU = 64
EPS = 1e-5
EPS_Z = 1e-10
TOTAL_INV = 1.0 / 32768.0

DEBUG = ()
STOP = None

ENGINES = ("sync", "tensor", "vector", "scalar", "gpsimd")


class Prog:

    def __init__(self):
        self.ops = {e: [] for e in ENGINES}
        self.cnt = {}
        self.pending_gate = None
        self.gate_done = set()

    def sem(self, name):
        self.cnt[name] = 0

    ENG_SEM = {"vector": "v", "scalar": "s", "gpsimd": "g"}

    def op(self, eng, fn, waits=(), inc=None, amt=1, dma=False):
        auto = inc is None and not dma and eng in self.ENG_SEM
        if auto:
            inc = self.ENG_SEM[eng]
        waits = list(waits)
        if auto and eng == "gpsimd" and self.cnt["g"] > 0:
            waits.append(("g", self.cnt["g"]))
        waits = [w for w in waits if w is not None]
        if self.pending_gate is not None and eng not in self.gate_done:
            waits.extend(self.pending_gate)
            self.gate_done.add(eng)
        for (s, v) in waits:
            assert v <= self.cnt[s], f"wait {s}>={v} but planned {self.cnt[s]}"
        ev = None
        if inc is not None:
            self.cnt[inc] += amt
            ev = (inc, self.cnt[inc])
        self.ops[eng].append((fn, tuple(waits), inc, amt))
        return ev

    def counts_event(self, *sems):
        return [(s, self.cnt[s]) for s in sems if self.cnt[s] > 0]

    def gate_after(self, evs):
        evs = [e for e in evs if e is not None]
        self.pending_gate = evs
        self.gate_done = set()


def _emit(engobj, oplist, semh):
    waited = {}
    for fn, waits, inc, amt in oplist:
        for (s, v) in waits:
            if waited.get(s, -1) < v:
                engobj.wait_ge(semh[s], v)
                waited[s] = v
        instr = fn(engobj)
        if inc is not None:
            instr.then_inc(semh[inc], amt)


def build(debug=(), stop=None, cc_mode="ar"):
    nc = bass.Bass("TRN2", target_bir_lowering=False, debug=False)
    p = Prog()
    for s in ("ldx", "ldx1", "ldx2", "ldx3",
              "ldw", "ldwv", "ldwv2", "ldg", "ldo", "ldo2", "ldo3", "ldo4",
              "ldc", "ldew0", "ldew1",
              "tq0", "tq1", "tq2", "tq3", "tp", "ldm", "tl0", "tl1",
              "ccs", "cca", "st", "std", "st2", "t", "v", "s", "g"):
        p.sem(s)

    d_x = nc.dram_tensor("x", [T, D], F32, kind="ExternalInput").ap()
    d_wqkv = nc.dram_tensor("wqkv", [D, 1536], F32, kind="ExternalInput").ap()
    d_wo = nc.dram_tensor("wo", [D, D], F32, kind="ExternalInput").ap()
    d_gw = nc.dram_tensor("gw", [D, E], F32, kind="ExternalInput").ap()
    d_ew = nc.dram_tensor("ew", [64, 128, 1024], F32, kind="ExternalInput").ap()
    d_eb = nc.dram_tensor("eb", [128, 64], F32, kind="ExternalInput").ap()
    d_gbr = nc.dram_tensor("gbr", [128, E], F32, kind="ExternalInput").ap()
    d_gmh = nc.dram_tensor("gmh", [128, 512], F32, kind="ExternalInput").ap()
    d_bth = nc.dram_tensor("bth", [128, 512], F32, kind="ExternalInput").ap()
    d_mask = nc.dram_tensor("maskc", [128, 128], BF16, kind="ExternalInput").ap()
    d_ident = nc.dram_tensor("identc", [128, 128], BF16, kind="ExternalInput").ap()
    d_onesc = nc.dram_tensor("onesc", [128, 1], F32, kind="ExternalInput").ap()
    d_onesr = nc.dram_tensor("onesr", [1, 128], F32, kind="ExternalInput").ap()
    d_out = nc.dram_tensor("out", [T, D], F32, kind="ExternalOutput").ap()
    d_ccin = nc.dram_tensor("cc_in", [1, E], F32, kind="Internal").ap()
    d_ccout = nc.dram_tensor("cc_out", [1, E], F32, kind="Internal",
                             addr_space="Shared").ap()
    d_dbg = {}
    for name, shape, dt in (
        ("dbg_ln1fm", [128, 16384], BF16), ("dbg_xnq", [128, 4096], BF16),
        ("dbg_xnk", [128, 4096], BF16), ("dbg_vtm", [128, 16640], BF16),
        ("dbg_ofm", [128, 16384], BF16), ("dbg_x1", [128, 16384], BF16),
        ("dbg_ln3fm", [128, 16384], BF16), ("dbg_cb", [128, 8], F32),
        ("dbg_ffnfm", [128, 16384], BF16), ("dbg_opair", [128, 4096], BF16),
        ("dbg_st0", [128, 130], BF16), ("dbg_oraw", [128, 65], F32),
        ("dbg_qffm", [128, 2048], BF16), ("dbg_stA", [128, 130], BF16),
        ("dbg_zr", [128, 1], F32), ("dbg_osl", [128, 64], F32),
    ):
        if name in debug:
            d_dbg[name] = nc.dram_tensor(name, shape, dt, kind="ExternalOutput").ap()

    def dbg(name, sbuf_ap, waits):
        if name in d_dbg:
            ev = p.op("sync", lambda sy, d=d_dbg[name], s0=sbuf_ap:
                      sy.dma_start(out=d, in_=s0), waits=list(waits),
                      inc="std", amt=16)
            p.gate_after([ev])

    def dbgng(name, sbuf_ap, waits):
        if name in d_dbg:
            p.op("sync", lambda sy, d=d_dbg[name], s0=sbuf_ap:
                 sy.dma_start(out=d, in_=s0), waits=list(waits),
                 inc="std", amt=16)

    def A(name, shape, dt):
        return nc.alloc_sbuf_tensor(name, shape, dt).ap()

    x_tm = A("x_tm", [128, NT * D], BF16)
    lnR = A("lnR", [128, 16384], BF16)
    wqkv = A("wqkv_sb", [128, 12288], BF16)
    wo = wqkv[:, 0:8192]
    gw = A("gw_sb", [128, 64], BF16)
    ewb = A("ew_sb", [128, 2048], BF16)
    xn_qk = A("xn_qk", [128, 8192], BF16)
    v_tm = A("v_tm", [128, NT * H * VW], BF16)
    qkf_tm = A("qkf_tm", [128, 8192], BF16)
    qkf_fm = A("qkf_fm", [128, 8192], BF16)
    opair = A("opair", [128, 4096], BF16)
    at_sb = A("at_sb", [128, 512], BF16)
    st_bf = A("state_bf", [128, 520], BF16)
    lnt = A("lnt", [128, 2048], BF16)
    ln4o = lnR[:, 0:2048]
    hsb = A("hs", [128, 4096], BF16)
    sqj = hsb

    f32s = A("f32s", [128, 1536], F32)
    stats = A("stats", [128, 256], F32)
    fmst = A("fmst", [128, 192], F32)
    gmh = A("gmh_sb", [128, 512], BF16)
    bth = A("bth_sb", [128, 512], BF16)
    gbr = A("gbr_sb", [128, E], F32)
    eb = A("eb_sb", [128, 64], F32)
    ebs = A("ebs_sb", [128, 64], F32)
    cb = A("cb_sb", [128, E], F32)
    onesc = A("onesc_sb", [128, 1], F32)
    onesr = A("onesr_sb", [1, 128], F32)
    cnt_l = A("cnt_l", [1, E], F32)
    cnt_g = A("cnt_g", [1, E], F32)
    gate_sb = A("gate_sb", [128, 16], F32)
    gsc = A("gsc", [128, 32], F32)
    zr = A("zr", [128, 8], F32)
    maskc = A("maskc_sb", [128, 128], BF16)
    identc = A("identc_sb", [128, 128], BF16)

    psum = nc.alloc_psum_tensor("psum", [128, 4096], F32).ap()

    def b(i, n=512):
        return psum[:, i * 512: i * 512 + n]

    def bb(i):
        return psum[:, i * 512: (i + 1) * 512].bitcast(BF16)

    ldxs = ["ldx", "ldx1", "ldx2", "ldx3"]
    ev_ldx = {}
    wq8 = wqkv.rearrange("p (c f) -> p c f", c=8)
    d_wq8 = d_wqkv.rearrange("(c p) f -> p c f", p=128)

    def ld_x(i):
        src = d_x[i * 512:(i + 1) * 512, :].rearrange("(t p) d -> p t d", p=128)
        dst = x_tm[:, i * 4096:(i + 1) * 4096].rearrange("p (t d) -> p t d", d=D)
        ev_ldx[i] = p.op("gpsimd", lambda g, s0=src, d0=dst: g.dma_start(
            out=d0, in_=s0), inc=ldxs[i], amt=16)

    ld_x(0)
    ev_wqkv = p.op("gpsimd", lambda g: g.dma_start(
        out=wq8[:, :, 0:512], in_=d_wq8[:, :, 0:512]), inc="ldw", amt=16)
    for dst, src in ((gmh, d_gmh), (bth, d_bth)):
        ev_consts2 = p.op("gpsimd", lambda g, d=dst, s0=src: g.dma_start(
            out=d, in_=s0), inc="ldm", amt=16)
    ev_wqkv_v = p.op("gpsimd", lambda g: g.dma_start(
        out=wq8[:, 0:4, 512:1536], in_=d_wq8[:, 0:4, 512:1536]),
        inc="ldwv", amt=16)
    ld_x(1)
    ev_wqkv_v2 = p.op("gpsimd", lambda g: g.dma_start(
        out=wq8[:, 4:8, 512:1536], in_=d_wq8[:, 4:8, 512:1536]),
        inc="ldwv2", amt=16)
    ld_x(2)
    ld_x(3)
    ev_ew = {}
    ev_consts = None
    for dst, src in ((gbr, d_gbr), (maskc, d_mask),
                     (identc, d_ident), (onesc, d_onesc), (onesr, d_onesr),
                     (eb, d_eb)):
        ev_consts = p.op("sync", lambda sy, d=dst, s0=src: sy.dma_start(
            out=d, in_=s0), inc="ldc", amt=16)
    ev_gm = p.op("vector", lambda v: v.tensor_scalar_mul(gmh, gmh, 0.5),
                 waits=[ev_consts, ("ldm", 32)])
    ev_gb = p.op("vector", lambda v: v.tensor_scalar_mul(bth, bth, 0.5),
                 waits=[ev_gm])

    sq_last = [None]

    def sq_war():
        return [sq_last[0]] if sq_last[0] is not None else []

    def emit_ln(ln_id, t, src_ap, out_ap, extra_waits=()):
        base = ln_id * 64
        s1 = stats[:, base + t: base + t + 1]
        s2 = stats[:, base + 16 + t: base + 17 + t]
        rstd = stats[:, base + 32 + t: base + 33 + t]
        nmr = stats[:, base + 48 + t: base + 49 + t]
        w = list(extra_waits)
        ev_r = p.op("vector", lambda v, s1=s1, sa=src_ap: v.tensor_reduce(
            s1, sa, AX.X, ALU.add), waits=w)
        wq_ = list(extra_waits) + list(sq_war()) + [ev_r]
        ev_q = p.op("scalar", lambda sc, s2=s2, sa=src_ap: sc.activation(
            sqj[:, 0:1024], sa, AF.Square, accum_out=s2), waits=wq_)
        sq_last[0] = ev_q
        ev_m = p.op("vector", lambda v, s1=s1: v.tensor_scalar_mul(
            s1, s1, 1.0 / D), waits=[ev_r])
        ev_va = p.op("vector", lambda v, rstd=rstd, s2=s2: v.tensor_scalar(
            rstd, s2, 1.0 / D, EPS, ALU.mult, ALU.add), waits=[ev_q])
        ev_n0 = p.op("vector", lambda v, nmr=nmr, s1=s1: v.scalar_tensor_tensor(
            nmr, s1, -1.0, s1, ALU.mult, ALU.mult), waits=[ev_m])
        ev_ad = p.op("vector", lambda v, rstd=rstd, nmr=nmr: v.tensor_tensor(
            rstd, rstd, nmr, ALU.add), waits=[ev_va, ev_n0])
        ev_rc = p.op("scalar", lambda sc, rstd=rstd: sc.activation(
            rstd, rstd, AF.Ln), waits=[ev_ad])
        ev_sq = p.op("scalar", lambda sc, rstd=rstd: sc.activation(
            rstd, rstd, AF.Exp, scale=-0.5), waits=[ev_rc])
        ev_nm = p.op("vector", lambda v, nmr=nmr, s1=s1, rstd=rstd:
                     v.scalar_tensor_tensor(nmr, s1, -1.0, rstd, ALU.mult,
                                            ALU.mult), waits=[ev_sq, ev_n0])
        ev_ap = p.op("scalar", lambda sc, o=out_ap, sa=src_ap, rstd=rstd, nmr=nmr:
                     sc.activation(o, sa, AF.Identity, scale=rstd, bias=nmr),
                     waits=[ev_nm])
        return ev_ap

    def emit_ln_batch(ln_id, src16, extra_waits=()):
        base = ln_id * 64
        s1a = stats[:, base:base + 16]
        s2a = stats[:, base + 16:base + 32]
        rstda = stats[:, base + 32:base + 48]
        nmra = stats[:, base + 48:base + 64]
        w = list(extra_waits)
        src3 = src16.rearrange("p (t d) -> p t d", d=D)
        ev_r = p.op("vector", lambda v: v.tensor_reduce(
            s1a, src3, AX.X, ALU.add), waits=w)
        ev_q = None
        for t in range(NT):
            wq_ = list(w) + list(sq_war()) if t == 0 else []
            ev_q = p.op("scalar", lambda sc, t=t: sc.activation(
                sqj[:, 0:1024], src16[:, t * D:(t + 1) * D], AF.Square,
                accum_out=s2a[:, t:t + 1]),
                waits=wq_ + ([ev_q] if ev_q else []))
        sq_last[0] = ev_q
        ev_m = p.op("vector", lambda v: v.tensor_scalar_mul(
            s1a, s1a, 1.0 / D), waits=[ev_r])
        ev_va = p.op("vector", lambda v: v.tensor_scalar(
            rstda, s2a, 1.0 / D, EPS, ALU.mult, ALU.add), waits=[ev_q])
        ev_n0 = p.op("vector", lambda v: v.scalar_tensor_tensor(
            nmra, s1a, -1.0, s1a, ALU.mult, ALU.mult), waits=[ev_m])
        ev_ad = p.op("vector", lambda v: v.tensor_tensor(
            rstda, rstda, nmra, ALU.add), waits=[ev_va, ev_n0])
        ev_rc = p.op("scalar", lambda sc: sc.activation(
            rstda, rstda, AF.Ln), waits=[ev_ad])
        ev_sq = p.op("scalar", lambda sc: sc.activation(
            rstda, rstda, AF.Exp, scale=-0.5), waits=[ev_rc])
        ev_nm = p.op("vector", lambda v: v.scalar_tensor_tensor(
            nmra, s1a, -1.0, rstda, ALU.mult, ALU.mult), waits=[ev_sq, ev_n0])
        return rstda, nmra, ev_nm

    ev_qkevac = {}
    ev_v = {}
    ev_xn = {}
    ev_fm_ap = {}
    ev_trdma = {}
    ev_s2red = {}
    ev_qkmm = {}
    ev_vmm = {}
    s1g = stats[:, 0:16]
    s2g = stats[:, 16:32]
    rstd1 = stats[:, 32:48]
    nmr1 = stats[:, 48:64]
    tail1 = {}
    ev_q1 = {}
    ev_r1 = {}

    def emit_ln1_stats(g):
        src3g = x_tm[:, g * 4096:(g + 1) * 4096].rearrange(
            "p (t d) -> p t d", d=D)
        ev_r1[g] = p.op("vector", lambda v, g=g, src3g=src3g: v.tensor_reduce(
            s1g[:, g * 4:g * 4 + 4], src3g, AX.X, ALU.add),
            waits=[(ldxs[g], 16)])
        for t in range(g * 4, g * 4 + 4):
            qw = [(ldxs[g], 16)] + list(sq_war())
            ev_q1[t] = p.op("scalar", lambda sc, t=t: sc.activation(
                sqj[:, 0:1024], x_tm[:, t * D:(t + 1) * D], AF.Square,
                accum_out=s2g[:, t:t + 1]), waits=qw)
            sq_last[0] = ev_q1[t]
        c0 = g * 4
        s1h = s1g[:, c0:c0 + 4]
        s2h = s2g[:, c0:c0 + 4]
        rsh = rstd1[:, c0:c0 + 4]
        nmh = nmr1[:, c0:c0 + 4]
        ev_m = p.op("vector", lambda v, s1h=s1h: v.tensor_scalar_mul(
            s1h, s1h, 1.0 / D), waits=[ev_r1[g]])
        ev_va = p.op("vector", lambda v, rsh=rsh, s2h=s2h: v.tensor_scalar(
            rsh, s2h, 1.0 / D, EPS, ALU.mult, ALU.add), waits=[ev_q1[c0 + 3]])
        ev_n0 = p.op("vector", lambda v, nmh=nmh, s1h=s1h:
                     v.scalar_tensor_tensor(
            nmh, s1h, -1.0, s1h, ALU.mult, ALU.mult), waits=[ev_m])
        ev_ad = p.op("vector", lambda v, rsh=rsh, nmh=nmh: v.tensor_tensor(
            rsh, rsh, nmh, ALU.add), waits=[ev_va, ev_n0])
        ev_rc = p.op("scalar", lambda sc, rsh=rsh: sc.activation(
            rsh, rsh, AF.Ln), waits=[ev_ad])
        ev_sq = p.op("scalar", lambda sc, rsh=rsh: sc.activation(
            rsh, rsh, AF.Exp, scale=-0.5), waits=[ev_rc])
        tail1[g] = p.op("vector", lambda v, nmh=nmh, s1h=s1h, rsh=rsh:
                        v.scalar_tensor_tensor(nmh, s1h, -1.0, rsh,
                                               ALU.mult, ALU.mult),
                        waits=[ev_sq, ev_n0])

    ld_gate = [("ldx", 16), ("ldx1", 16), ("ldx2", 16), ("ldx3", 16),
               ("ldw", 16), ("ldwv", 16), ("ldwv2", 16), ("ldm", 32),
               ("ldc", p.cnt["ldc"])]
    ev_trP1 = {}

    def emit_ln_tr(t):
        if t // 4 not in tail1:
            emit_ln1_stats(t // 4)
        xt = x_tm[:, t * D:(t + 1) * D]
        ls = (t % 2) * 1024
        lslot = lnt[:, ls:ls + 1024]
        w = [tail1[t // 4]]
        if t >= 2:
            w.append(ev_trP1[t - 2])
        ev_ap = p.op("scalar", lambda sc, o=lslot, sa=xt, t=t: sc.activation(
            o, sa, AF.Identity, scale=rstd1[:, t:t + 1],
            bias=nmr1[:, t:t + 1]), waits=w)
        tb = 3 + (t % 2)
        trw = [ev_ap]
        if t == 0:
            trw.append(ev_consts)
        if t >= 2:
            trw.append(ev_trdma[t - 2])
        for dc in range(DC):
            ev_trP1[t] = p.op("tensor", lambda te, dc=dc, ls=ls, tb=tb:
                              te.transpose(
                bb(tb)[:, dc * 128:(dc + 1) * 128],
                in_=lnt[:, ls + dc * 128: ls + (dc + 1) * 128],
                identity=identc),
                waits=trw if dc == 0 else (),
                inc="t" if dc == DC - 1 else None)
    def emit_tr_evac(t):
        tb = 3 + (t % 2)
        dstv = lnR.rearrange("p (c n) -> p c n", c=8)[:, :, t * 128:(t + 1) * 128]
        srcv = bb(tb).rearrange("p (c n) -> p c n", c=8)
        ev_trdma[t] = p.op("vector", lambda v, d=dstv, s0=srcv:
                           v.tensor_copy(d, s0), waits=[ev_trP1[t]])

    def emit_proj_fm(tc):
        qkb = b(5)
        vb = b(6)
        mmw = [ev_trdma[tc], ev_wqkv]
        if tc >= 1:
            mmw.append(ev_vc1_d[tc - 1])
        first = True
        for col0, wd, dstp in ((0, 256, qkb[:, 0:256]),
                               (256, 256, qkb[:, 256:512])):
            for dc in range(DC):
                ev_qkmm[tc] = p.op("tensor", lambda te, dc=dc, c0=col0, wd=wd,
                                   dp=dstp, tc=tc: te.matmul(
                    dp, lhsT=lnR[:, dc * 2048 + tc * 128: dc * 2048 + (tc + 1) * 128],
                    rhs=wqkv[:, dc * 1536 + c0: dc * 1536 + c0 + wd],
                    start=(dc == 0), stop=(dc == DC - 1)),
                    waits=mmw if first else (),
                    inc="t" if (dc == DC - 1 and col0 == 256) else None)
                first = False
        vdst = v_tm[:, tc * H * VW:(tc + 1) * H * VW].rearrange(
            "p (h w) -> p h w", w=VW)
        wv0 = [ev_vc0_d[tc - 1]] if tc >= 1 else [ev_wqkv_v]
        ev_vmm0 = None
        for dc in range(DC):
            wvd = wv0 if dc == 0 else (
                [("ldwv2", 16)] if (dc == 4 and tc == 0) else ())
            ev_vmm0 = p.op("tensor", lambda te, dc=dc, tc=tc: te.matmul(
                vb, lhsT=lnR[:, dc * 2048 + tc * 128: dc * 2048 + (tc + 1) * 128],
                rhs=wqkv[:, dc * 1536 + 512: dc * 1536 + 1024],
                start=(dc == 0), stop=(dc == DC - 1)),
                waits=wvd,
                inc="t" if dc == DC - 1 else None)

        qk = f32s[:, (tc % 3) * 512: (tc % 3) * 512 + 512]
        evw = [ev_qkmm[tc]]
        if tc >= 3:
            evw.append(ev_fm_ap[tc - 3])
        ev_qkevac[tc] = p.op("scalar", lambda sc, qk=qk, qkb=qkb: sc.copy(
            qk, qkb), waits=evw)

        ev_vmm1 = None
        for dc in range(DC):
            ev_vmm1 = p.op("tensor", lambda te, dc=dc, tc=tc: te.matmul(
                qkb, lhsT=lnR[:, dc * 2048 + tc * 128: dc * 2048 + (tc + 1) * 128],
                rhs=wqkv[:, dc * 1536 + 1024: dc * 1536 + 1536],
                start=(dc == 0), stop=(dc == DC - 1)),
                waits=[ev_qkevac[tc]] if dc == 0 else (),
                inc="t" if dc == DC - 1 else None)
        vs0 = vb.rearrange("p (h w) -> p h w", w=64)
        vs1 = qkb.rearrange("p (h w) -> p h w", w=64)
        ev_vc0_d[tc] = p.op("scalar", lambda sc, d=vdst, s0=vs0: sc.copy(
            d[:, 0:8, 0:64], s0), waits=[ev_vmm0])
        if tc + 2 <= NT - 1 and lntr_next[0] <= tc + 2:
            emit_ln_tr(tc + 2)
            lntr_next[0] += 1
        ev_vc1_d[tc] = p.op("vector", lambda v, d=vdst, s0=vs1: v.tensor_copy(
            d[:, 8:16, 0:64], s0), waits=[ev_vmm1])
        ev_v[tc] = p.op("vector", lambda v, d=vdst: v.memset(d[:, :, 64:65], 1.0),
                        waits=[ev_vc0_d[tc], ev_vc1_d[tc]])
        if tc + 1 <= 8 and tc + 1 not in ev_trdma and tc + 1 in ev_trP1:
            emit_tr_evac(tc + 1)

        qk3 = qk.rearrange("p (a c) -> p a c", c=FD)
        c0, c1 = (tc % 3) * 32, 96 + (tc % 3) * 32
        wred = [ev_qkevac[tc]]
        if tc >= 3:
            wred.append(ev_fm_ap[tc - 3])
        ev_red1 = p.op("vector", lambda v, qk3=qk3, c0=c0: v.tensor_reduce(
            fmst[:, c0:c0 + 32], qk3, AX.X, ALU.add), waits=wred)
        ev_mu = p.op("vector", lambda v, c0=c0: v.tensor_scalar_mul(
            fmst[:, c0:c0 + 32], fmst[:, c0:c0 + 32], 1.0 / FD),
            waits=[ev_red1])
        mu_b = fmst[:, c0:c0 + 32][:, :, None].broadcast_to([128, 32, FD])
        rs_b = fmst[:, c1:c1 + 32][:, :, None].broadcast_to([128, 32, FD])
        apw = [ev_mu]
        if tc == 0:
            apw += [ev_gm, ev_gb]
        ev_a = p.op("vector", lambda v, qk3=qk3, mu_b=mu_b: v.tensor_tensor(
            qk3, qk3, mu_b, ALU.subtract), waits=apw)
        sqs = sqj[:, 1024 + (tc % 3) * 512:1536 + (tc % 3) * 512]
        fw2 = [ev_a]
        if tc >= 3:
            fw2.append(ev_s2red[tc - 3])
        ev_fsq = p.op("scalar", lambda sc, qk=qk, sqs=sqs: sc.activation(
            sqs, qk, AF.Square), waits=fw2)
        sq3 = sqs.rearrange("p (a c) -> p a c", c=FD)
        ev_s2red[tc] = p.op("vector", lambda v, sq3=sq3, c1=c1: v.tensor_reduce(
            fmst[:, c1:c1 + 32], sq3, AX.X, ALU.add), waits=[ev_fsq])
        ev_va = p.op("vector", lambda v, c1=c1: v.tensor_scalar(
            fmst[:, c1:c1 + 32], fmst[:, c1:c1 + 32], 1.0 / FD, EPS,
            ALU.mult, ALU.add), waits=[ev_s2red[tc]])
        ev_frc = p.op("scalar", lambda sc, c1=c1: sc.activation(
            fmst[:, c1:c1 + 32], fmst[:, c1:c1 + 32], AF.Ln), waits=[ev_va])
        ev_fsqr = p.op("scalar", lambda sc, c1=c1: sc.activation(
            fmst[:, c1:c1 + 32], fmst[:, c1:c1 + 32], AF.Exp, scale=-0.5),
            waits=[ev_frc])
        ev_b = p.op("vector", lambda v, qk3=qk3, rs_b=rs_b: v.tensor_tensor(
            qk3, qk3, rs_b, ALU.mult), waits=[ev_fsqr])
        ev_c = p.op("vector", lambda v, qk=qk: v.tensor_tensor(
            qk, qk, gmh, ALU.mult), waits=[ev_b])
        ev_fm_ap[tc] = ev_xn[tc] = p.op(
            "vector", lambda v, qk=qk, tc=tc: v.tensor_tensor(
                xn_qk[:, tc * 512:(tc + 1) * 512], qk, bth, ALU.add),
            waits=[ev_c])

    ev_vc0_d = {}
    ev_vc1_d = {}
    lntr_next = [0]

    def emit_tile(t):
        while lntr_next[0] <= min(t + 1, NT - 1):
            emit_ln_tr(lntr_next[0])
            lntr_next[0] += 1
        if t not in ev_trdma:
            emit_tr_evac(t)
        emit_proj_fm(t)

    for t in range(8):
        emit_tile(t)
    for t in range(8, NT):
        if t not in ev_trdma:
            emit_tr_evac(t)
        while lntr_next[0] <= min(t + 2, NT - 1):
            emit_ln_tr(lntr_next[0])
            lntr_next[0] += 1
    ev_p2_t = None

    if stop == "p2":
        for t in range(8, NT):
            emit_tile(t)
        dbg("dbg_ln1fm", lnR,
            [("tl0", p.cnt["tl0"]), ("tl1", p.cnt["tl1"])])
        dbg("dbg_vtm", v_tm, [ev_v[NT - 1]])
        p.op("gpsimd", lambda g: g.nop(),
             waits=[("st", p.cnt["st"]), ("std", p.cnt["std"])])
        return _finish(nc, p)

    ev_kr = {}
    ev_trq = {}
    ev_trk = {}
    ev_evq = {}
    ev_evk = {}
    ev_pe_done = {}
    ev_mask_all = {}
    ev_omm_all = {}
    ev_oev_all = {}
    ev_stmm_all = {}
    ev_stev_all = {}
    ev_pair_tr = {}
    ev_pair_ev = {}
    ev_at2_all = {}

    worder = [(pr, sq, hp) for pr in range(8) for sq in range(4)
              for hp in range(2)]

    def whs(w):
        pr, sq, hp = worder[w]
        return 2 * pr + hp, sq, hp

    xn3 = xn_qk.rearrange("p (g x) -> p g x", x=512)

    def kr_ops(w):
        h, sq, hp = whs(w)
        pg = (w % 4) * 1024
        gc0 = sq * 4
        evs = []
        for koff, dof, eng, sm in ((256, 1024, "gpsimd", "g"),
                                   (0, 0, "gpsimd", "g")):
            w_ = [ev_xn[gc0 + 3]]
            if w >= 4:
                w_.append(ev_pe_done[w - 4])
                w_.append(ev_evq.get(w - 4))
                w_.append(ev_evk.get(w - 4))
            xs3 = xn3[:, gc0:gc0 + 4, koff + h * FD: koff + (h + 1) * FD]
            o4 = qkf_tm[:, 2 * pg + dof: 2 * pg + dof + 1024].rearrange(
                "p (c a b) -> p c a b", a=FD, b=FD)
            i0 = xs3[:, :, :, None].broadcast_to([128, 4, FD, FD])
            i1 = xs3[:, :, None, :].broadcast_to([128, 4, FD, FD])
            evs.append(p.op(eng, lambda en, o4=o4, i0=i0, i1=i1:
                            en.tensor_tensor(o4, i0, i1, ALU.mult),
                            waits=w_, inc=sm))
        return evs

    def dma_tq(w, extra=()):
        h, sq, hp = whs(w)
        pgt = (w % 4) * 2048
        pgf = (w % 4) * 2048
        w_ = [ev_kr[w][1], ev_kr[w][0]] + list(extra)
        if w >= 4:
            w_.append(ev_pe_done[w - 4])
        dst = qkf_fm[:, pgf:pgf + 2048].rearrange("p (m q) -> p m q", q=128)
        return p.op("sync", lambda sy, dst=dst, pgt=pgt: sy.dma_start(
            out=dst, in_=qkf_tm[:, pgt:pgt + 2048], transpose=True),
            waits=w_, inc=f"tq{w % 4}", amt=16)

    nu_stop = NU
    if stop and str(stop).startswith("attn") and len(str(stop)) > 4:
        nu_stop = int(str(stop)[4:])

    ev_kr[0] = kr_ops(0)
    ev_kr[1] = kr_ops(1)
    ev_evq[0] = ev_evk[0] = dma_tq(0, extra=ld_gate)
    ev_evq[1] = ev_evk[1] = dma_tq(1)

    def mkctx(w):
        h, sq, hp = whs(w)
        return dict(w=w, h=h, sq=sq, hp=hp, pr=h // 2, pp=(h // 2) % 2,
                    pg=(w % 4) * 1024, pgt=(w % 4) * 1024,
                    sb=0 if hp == 0 else 7, ob=1 + hp,
                    ab=3 + hp, stoff=hp * 260)

    def do_attT(cx, c):
        h, sq, hp, pg, ab = cx["h"], cx["sq"], cx["hp"], cx["pg"], cx["ab"]
        gc = sq * 4 + c
        w_ = [ev_evq[cx["w"]], ev_evk[cx["w"]]] if c == 0 else []
        if cx["w"] < 2 and c == 0:
            w_.append(ev_trdma[14 + cx["hp"]])
        keym = (h, gc - 1) if gc >= 1 else (h - 2, 15)
        if keym in ev_mask_all:
            w_.append(ev_mask_all[keym])
        if keym in ev_at2_all:
            w_.append(ev_at2_all[keym])
        ev = None
        for fc in range(2):
            ev = p.op("tensor", lambda te, c=c, fc=fc, ab=ab, pg=pg:
                      te.matmul(
                b(ab, 128),
                lhsT=qkf_fm[:, 2 * pg + 1024 + (c * 2 + fc) * 128:
                            2 * pg + 1024 + (c * 2 + fc + 1) * 128],
                rhs=qkf_fm[:, 2 * pg + (c * 2 + fc) * 128:
                           2 * pg + (c * 2 + fc + 1) * 128],
                start=(fc == 0), stop=(fc == 1 and c == 0)),
                waits=w_ if fc == 0 else (),
                inc="t" if fc == 1 else None)
        if c >= 1:
            for fc in range(2):
                ev = p.op("tensor", lambda te, c=c, fc=fc, ab=ab, pg=pg:
                          te.matmul(
                    b(ab)[:, 128:256],
                    lhsT=qkf_fm[:, 2 * pg + 1024 + ((c - 1) * 2 + fc) * 128:
                                2 * pg + 1024 + ((c - 1) * 2 + fc + 1) * 128],
                    rhs=qkf_fm[:, 2 * pg + (c * 2 + fc) * 128:
                               2 * pg + (c * 2 + fc + 1) * 128],
                    start=False, stop=(fc == 1)),
                    inc="t" if fc == 1 else None)
        return ev

    def do_mask(cx, c, ev_attT):
        h, sq, hp, ab = cx["h"], cx["sq"], cx["hp"], cx["ab"]
        gc = sq * 4 + c
        dstm = at_sb[:, hp * 256: hp * 256 + 128]
        w_ = [ev_attT]
        if cx["w"] == 0 and c == 0:
            w_.append(ev_consts)
        keyo = (h, gc - 1) if gc >= 1 else (h - 2, 15)
        if keyo in ev_omm_all:
            w_.append(ev_omm_all[keyo])
        ev_mask_all[(h, gc)] = p.op(
            "vector", lambda v, dstm=dstm, ab=ab:
            v.tensor_tensor(dstm, b(ab, 128), maskc, ALU.mult),
            waits=w_, inc="v")
        if c >= 1:
            dst2 = at_sb[:, hp * 256 + 128: hp * 256 + 256]
            ev_at2_all[(h, gc)] = p.op(
                "scalar", lambda sc, dst2=dst2, ab=ab:
                sc.copy(dst2, b(ab)[:, 128:256]), waits=[ev_attT], inc="s")

    def do_oz(cx, c):
        h, sq, hp, pg = cx["h"], cx["sq"], cx["hp"], cx["pg"]
        pgt = cx["pgt"]
        sb, ob, stoff, pr, pp = (cx["sb"], cx["ob"], cx["stoff"], cx["pr"],
                                 cx["pp"])
        gc = sq * 4 + c
        a_ap = at_sb[:, hp * 256: hp * 256 + 128]
        vsl = v_tm[:, gc * H * VW + h * VW: gc * H * VW + (h + 1) * VW]
        w_ = [ev_mask_all[(h, gc)], ev_v[gc]]
        use_mid = (c >= 1)
        if use_mid:
            w_.append(ev_at2_all[(h, gc)])
            if gc >= 2:
                w_.append(ev_stev_all[(h, gc - 2)])
            if gc >= 1:
                w_.append(ev_v[gc - 1])
        elif gc > 0:
            w_.append(ev_stev_all[(h, gc - 1)])
        keyo = (h, gc - 1) if gc >= 1 else (h - 2, 15)
        if keyo in ev_oev_all:
            w_.append(ev_oev_all[keyo])
        n_inter = 0 if (gc == 0 or (use_mid and gc < 2)) else 2
        n_tot = 1 + (1 if use_mid else 0) + n_inter
        k = 1
        ev_omm = p.op("tensor",
                      lambda te, a_ap=a_ap, vsl=vsl, ob=ob, single=(n_tot == 1):
                      te.matmul(b(ob, VW), lhsT=a_ap, rhs=vsl, start=True,
                                stop=single), waits=w_,
                      inc="t" if n_tot == 1 else None)
        if use_mid:
            k += 1
            a2 = at_sb[:, hp * 256 + 128: hp * 256 + 256]
            v2 = v_tm[:, (gc - 1) * H * VW + h * VW:
                      (gc - 1) * H * VW + (h + 1) * VW]
            ev_omm = p.op("tensor",
                          lambda te, a2=a2, v2=v2, ob=ob, last=(k == n_tot):
                          te.matmul(b(ob, VW), lhsT=a2, rhs=v2, start=False,
                                    stop=last),
                          inc="t" if k == n_tot else None)
        if n_inter:
            soff = stoff + ((gc % 2) if use_mid else ((gc - 1) % 2)) * 130
            for fc in range(2):
                k += 1
                ev_omm = p.op("tensor",
                              lambda te, c=c, fc=fc, ob=ob, pg=pg,
                              soff=soff, last=(k == n_tot): te.matmul(
                    b(ob, VW),
                    lhsT=qkf_fm[:, 2 * pg + (c * 2 + fc) * 128:
                                2 * pg + (c * 2 + fc + 1) * 128],
                    rhs=st_bf[:, soff + fc * VW: soff + (fc + 1) * VW],
                    start=False, stop=last),
                    inc="t" if k == n_tot else None)
        ev_omm_all[(h, gc)] = ev_omm
        if gc < 15:
            wst = [ev_v[gc]]
            if c == 0:
                wst.append(ev_kr[cx["w"]][0])
            if gc > 0:
                wst.append(ev_stev_all[(h, gc - 1)])
            else:
                if (h - 2, 14) in ev_stev_all:
                    wst.append(ev_stev_all[(h - 2, 14)])
                if hp == 1 and (pr - 1) in ev_pair_ev:
                    wst.append(ev_pair_ev[pr - 1])
            ev_stmm = None
            for fc in range(2):
                ev_stmm = p.op("tensor",
                               lambda te, c=c, fc=fc, pg=pgt, gc=gc, h=h,
                               sb=sb: te.matmul(
                    b(sb)[:, fc * 128: fc * 128 + VW],
                    lhsT=qkf_tm[:, 2 * pg + 1024 + c * 256 + fc * 128:
                                2 * pg + 1024 + c * 256 + fc * 128 + 128],
                    rhs=v_tm[:, gc * H * VW + h * VW:
                             gc * H * VW + (h + 1) * VW],
                    start=(gc == 0 and fc == 0), stop=(gc == 14),
                    skip_group_check=True),
                    waits=wst if fc == 0 else (),
                    inc="t" if fc == 1 else None)
            ev_stmm_all[(h, gc)] = ev_stmm
            src2 = b(sb, 256).rearrange("p (f q) -> p f q", f=2)[:, :, 0:VW]
            so2 = stoff + (gc % 2) * 130
            dst2 = st_bf[:, so2:so2 + 130].rearrange(
                "p (f q) -> p f q", q=VW)
            ev_stev_all[(h, gc)] = p.op(
                "scalar", lambda sc, s0=src2, d0=dst2: sc.copy(d0, s0),
                waits=[ev_stmm, ev_omm_all[(h, gc)]], inc="s")
        zslot = hp * 2 + gc % 2
        zraw = zr[:, zslot:zslot + 1]
        zrec = zr[:, 4 + zslot: 5 + zslot]
        wz = [ev_omm_all[(h, gc)]]
        keyz = (h, gc - 2) if gc >= 2 else (h - 2, gc + 14)
        if keyz in ev_oev_all:
            wz.append(ev_oev_all[keyz])
        ev_z0 = p.op("vector", lambda v, zraw=zraw, ob=ob:
                     v.tensor_scalar_add(
            zraw, b(ob, VW)[:, 64:65], EPS_Z), waits=wz)
        ev_zr = p.op("vector", lambda v, zrec=zrec, zraw=zraw:
                     v.reciprocal(zrec, zraw), waits=[ev_z0])
        dsto = opair[:, pp * 2048 + gc * 128 + hp * 64:
                     pp * 2048 + gc * 128 + hp * 64 + 64]
        wo_ = [ev_zr]
        if hp == 0 and gc == 0 and (pr - 2) in ev_pair_tr:
            wo_.append(ev_pair_tr[pr - 2])
        ev_oev_all[(h, gc)] = p.op(
            "scalar", lambda sc, dsto=dsto, ob=ob, zrec=zrec:
            sc.activation(dsto, b(ob, VW)[:, 0:64], AF.Identity,
                          scale=zrec),
            waits=wo_, inc="s")

    nv_stop = (nu_stop + 1) // 2
    for v in range(nv_stop):
        if v == 1:
            for t in range(8, 12):
                emit_tile(t)
        elif v == 2:
            for t in range(12, 16):
                emit_tile(t)
            ev_p2_t = ("t", p.cnt["t"])
        wA, wB = 2 * v, 2 * v + 1
        A, B = mkctx(wA), mkctx(wB)
        pr, sq = worder[wA][0], worder[wA][1]

        e = do_attT(A, 0); do_mask(A, 0, e)
        e = do_attT(B, 0); do_mask(B, 0, e)
        if wA + 2 < NU:
            ev_kr[wA + 2] = kr_ops(wA + 2)
            ev_evq[wA + 2] = ev_evk[wA + 2] = dma_tq(wA + 2)
        do_oz(A, 0)
        e = do_attT(A, 1); do_mask(A, 1, e)
        do_oz(B, 0)
        e = do_attT(B, 1); do_mask(B, 1, e)
        do_oz(A, 1)
        e = do_attT(A, 2); do_mask(A, 2, e)
        if wB + 2 < NU:
            ev_kr[wB + 2] = kr_ops(wB + 2)
            ev_evq[wB + 2] = ev_evk[wB + 2] = dma_tq(wB + 2)
        do_oz(B, 1)
        e = do_attT(B, 2); do_mask(B, 2, e)
        do_oz(A, 2)
        e = do_attT(A, 3); do_mask(A, 3, e)
        do_oz(B, 2)
        e = do_attT(B, 3); do_mask(B, 3, e)
        do_oz(A, 3)
        ev_pe_done[wA] = (ev_omm_all[(A["h"], 15)] if sq == 3
                          else ev_stmm_all[(A["h"], sq * 4 + 3)])
        do_oz(B, 3)
        ev_pe_done[wB] = (ev_omm_all[(B["h"], 15)] if sq == 3
                          else ev_stmm_all[(B["h"], sq * 4 + 3)])

        if sq == 3:
            h = B["h"]
            pp = B["pp"]
            w_ = [ev_oev_all[(h, 15)]]
            if (pr - 2) in ev_pair_ev:
                w_.append(ev_pair_ev[pr - 2])
            if pr == 0:
                w_.append(ev_p2_t)
            dst = lnR[:, pr * 2048:(pr + 1) * 2048].rearrange(
                "p (m q) -> p m q", q=128)
            if pr < 7:
                ev_pair_ev[pr] = p.op("sync", lambda sy, dst=dst, pp=pp:
                                      sy.dma_start(
                    out=dst, in_=opair[:, pp * 2048:(pp + 1) * 2048],
                    transpose=True), waits=w_, inc="tp", amt=16)
                ev_pair_tr[pr] = ev_pair_ev[pr]
            else:
                w_.append(ev_oev_all[(h - 1, 15)])
                ev_t7 = None
                for blk in range(16):
                    tb7 = 1 + blk // 8
                    ev_t7 = p.op("tensor", lambda te, blk=blk, tb7=tb7,
                                 pp=pp: te.transpose(
                        bb(tb7)[:, (blk % 8) * 128:(blk % 8 + 1) * 128],
                        in_=opair[:, pp * 2048 + blk * 128:
                                  pp * 2048 + (blk + 1) * 128],
                        identity=identc),
                        waits=w_ if blk == 0 else (),
                        inc="t" if blk == 15 else None)
                ev_e7a = p.op("vector", lambda v, dst=dst: v.tensor_copy(
                    dst[:, 0:8, :],
                    bb(1).rearrange("p (m q) -> p m q", q=128)),
                    waits=[ev_t7], inc="v")
                ev_pair_ev[pr] = p.op("vector", lambda v, dst=dst:
                                      v.tensor_copy(
                    dst[:, 8:16, :],
                    bb(2).rearrange("p (m q) -> p m q", q=128)),
                    waits=[ev_t7], inc="v")
                ev_pair_tr[pr] = ev_pair_ev[pr]

    w_ld = [ev_p2_t, ("tp", p.cnt["tp"]), ("tq0", p.cnt["tq0"]),
            ("tq1", p.cnt["tq1"]), ("tq2", p.cnt["tq2"]),
            ("tq3", p.cnt["tq3"])]
    wo8 = wo.rearrange("p (c j) -> p c j", c=8)
    d_wo8 = d_wo.rearrange("(c p) j -> p c j", p=128)
    ldos = ["ldo", "ldo2", "ldo3", "ldo4"]
    for h in range(4):
        p.op("gpsimd", lambda g, h=h: g.dma_start(
            out=wo8[:, h * 2:(h + 1) * 2], in_=d_wo8[:, h * 2:(h + 1) * 2]),
            waits=w_ld if h == 0 else (), inc=ldos[h], amt=16)
    ev_gw = p.op("gpsimd", lambda g: g.dma_start(
        out=gw.rearrange("p (c e) -> p c e", c=8),
        in_=d_gw.rearrange("(c p) e -> p c e", p=128)),
        inc="ldg", amt=16)
    for idx in range(2):
        ev_ew[idx] = p.op("gpsimd", lambda g, idx=idx: g.dma_start(
            out=ewb[:, (idx % 2) * 1024:(idx % 2 + 1) * 1024], in_=d_ew[idx]),
            inc=f"ldew{idx % 2}", amt=16)

    if ev_pair_ev:
        dbg("dbg_ofm", lnR, [ev_pair_ev[max(ev_pair_ev)]])
        dbg("dbg_opair", opair, [ev_pair_ev[max(ev_pair_ev)]])
    if stop and str(stop).startswith("attn"):
        p.op("gpsimd", lambda g: g.nop(),
             waits=[("st", p.cnt["st"]), ("std", p.cnt["std"])])
        return _finish(nc, p)

    gate5 = p.counts_event("v", "s", "g") + [("tp", p.cnt["tp"])]
    ev_wo_mm = {}
    ev_ln2 = {}
    ev_ln2ap = {}
    ev_sq3 = {}
    ev_r3 = {}
    ev_r2 = {}
    ev_q2 = {}
    ev_ap2 = {}
    tail2 = {}
    wo_tm = v_tm
    s1c = stats[:, 64:80]
    s2c = stats[:, 80:96]
    rstd2 = stats[:, 96:112]
    nmr2 = stats[:, 112:128]

    def p5_mm(tc):
        base = 0 if tc % 2 == 0 else 2
        pv = psum[:, base * 512: base * 512 + 1024]
        w = gate5 + [("ldo", 16)] + ([ev_ln2ap[tc - 2]] if tc >= 2 else [])
        first = True
        for jh in range(2):
            for mc in range(8):
                wmm = w if first else (
                    [({2: "ldo2", 4: "ldo3", 6: "ldo4"}[mc], 16)]
                    if (tc == 0 and jh == 0 and mc in (2, 4, 6))
                    else ())
                ev_wo_mm[tc] = p.op("tensor", lambda te, mc=mc, jh=jh, tc=tc,
                                    base=base: te.matmul(
                    b(base + jh),
                    lhsT=lnR[:, mc * 2048 + tc * 128: mc * 2048 + (tc + 1) * 128],
                    rhs=wo[:, mc * 1024 + jh * 512: mc * 1024 + (jh + 1) * 512],
                    start=(mc == 0), stop=(mc == 7)),
                    waits=wmm,
                    inc="t" if (mc == 7 and jh == 1) else None)
                first = False
        slot = wo_tm[:, tc * 1024:(tc + 1) * 1024]
        ew_ = [ev_wo_mm[tc]]
        if tc < 2:
            ew_.append(ev_pe_done[NU - 1])
        if tc % 2 == 0:
            ev_ln2ap[tc] = p.op("scalar", lambda sc, slot=slot, pv=pv:
                                sc.copy(slot, pv), waits=ew_)
        else:
            ev_ln2ap[tc] = p.op("vector", lambda v, slot=slot, pv=pv:
                                v.tensor_copy(slot, pv), waits=ew_)
        ev_r2[tc] = p.op("vector", lambda v, tc=tc, slot=slot: v.tensor_reduce(
            s1c[:, tc:tc + 1], slot, AX.X, ALU.add), waits=[ev_ln2ap[tc]])
        qw = [ev_ln2ap[tc]]
        if tc < 2:
            qw += [ev_fm_ap[15], ev_fm_ap[13]]
        ev_q2[tc] = p.op("scalar", lambda sc, tc=tc, slot=slot: sc.activation(
            f32s.bitcast(BF16)[:, (tc % 2) * 1024:(tc % 2) * 1024 + 1024],
            slot, AF.Square, accum_out=s2c[:, tc:tc + 1]), waits=qw)

    def p5_tail(h):
        c0 = h * 8
        s1h = s1c[:, c0:c0 + 8]
        s2h = s2c[:, c0:c0 + 8]
        rsh = rstd2[:, c0:c0 + 8]
        nmh = nmr2[:, c0:c0 + 8]
        ev_m = p.op("vector", lambda v, s1h=s1h: v.tensor_scalar_mul(
            s1h, s1h, 1.0 / D), waits=[ev_r2[c0 + 7]])
        ev_va = p.op("vector", lambda v, rsh=rsh, s2h=s2h: v.tensor_scalar(
            rsh, s2h, 1.0 / D, EPS, ALU.mult, ALU.add), waits=[ev_q2[c0 + 7]])
        ev_n0 = p.op("vector", lambda v, nmh=nmh, s1h=s1h:
                     v.scalar_tensor_tensor(
            nmh, s1h, -1.0, s1h, ALU.mult, ALU.mult), waits=[ev_m])
        ev_ad = p.op("vector", lambda v, rsh=rsh, nmh=nmh: v.tensor_tensor(
            rsh, rsh, nmh, ALU.add), waits=[ev_va, ev_n0])
        ev_rc = p.op("scalar", lambda sc, rsh=rsh: sc.activation(
            rsh, rsh, AF.Ln), waits=[ev_ad])
        ev_sq = p.op("scalar", lambda sc, rsh=rsh: sc.activation(
            rsh, rsh, AF.Exp, scale=-0.5), waits=[ev_rc])
        tail2[h] = p.op("vector", lambda v, nmh=nmh, s1h=s1h, rsh=rsh:
                        v.scalar_tensor_tensor(nmh, s1h, -1.0, rsh,
                                               ALU.mult, ALU.mult),
                        waits=[ev_sq, ev_n0])

    def p5_b(tc):
        slot = wo_tm[:, tc * 1024:(tc + 1) * 1024]
        ls4 = tc % 4
        if ls4 < 2:
            lslot = lnt[:, ls4 * 1024:(ls4 + 1) * 1024]
        else:
            lslot = hsb[:, (ls4 - 2) * 1024:(ls4 - 1) * 1024]
        aw = [tail2[tc // 8]]
        if tc < 2:
            aw.append(ev_trP1[14 + tc])
        if tc >= 4:
            aw.append(ev_ln2[tc - 4])
        ev_ap2[tc] = p.op("scalar", lambda sc, slot=slot, lslot=lslot, tc=tc:
                          sc.activation(
            lslot, slot, AF.Identity, scale=rstd2[:, tc:tc + 1],
            bias=nmr2[:, tc:tc + 1]), waits=aw)
        ev_ln2[tc] = p.op("vector", lambda v, tc=tc, lslot=lslot:
                          v.tensor_tensor(
            x_tm[:, tc * D:(tc + 1) * D], x_tm[:, tc * D:(tc + 1) * D],
            lslot, ALU.add), waits=[ev_ap2[tc]], inc="v")

    def p5_c(tc):
        sqw = [ev_ln2[tc]]
        if tc < 2:
            sqw.append(("tp", p.cnt["tp"]))
        else:
            sqw.append(ev_sq3[tc - 2])
        ev_sq3[tc] = p.op("scalar", lambda sc, tc=tc: sc.activation(
            opair[:, (tc % 2) * 1024:(tc % 2) * 1024 + 1024],
            x_tm[:, tc * D:(tc + 1) * D], AF.Square,
            accum_out=stats[:, 144 + tc:145 + tc]), waits=sqw)
        if tc % 4 == 3:
            g = tc // 4
            src3g = x_tm[:, g * 4096:(g + 1) * 4096].rearrange(
                "p (t d) -> p t d", d=D)
            ev_r3[g] = p.op("vector", lambda v, g=g, src3g=src3g:
                            v.tensor_reduce(
                stats[:, 128 + g * 4: 132 + g * 4], src3g, AX.X, ALU.add),
                waits=[ev_ln2[tc]])

    for tc in range(8):
        p5_mm(tc)
    p5_tail(0)
    for tc in range(4):
        p5_mm(tc + 8)
        p5_b(2 * tc)
        p5_b(2 * tc + 1)
    for tc in range(4):
        p5_mm(tc + 12)
        p5_c(2 * tc)
        p5_c(2 * tc + 1)


    ev_ln3fm = {}
    ev_tr3 = {}
    s1b = stats[:, 128:144]
    s2b = stats[:, 144:160]
    rstd3 = stats[:, 160:176]
    nmr3 = stats[:, 176:192]
    tail3 = {}

    def ln3_tail(h):
        c0 = h * 8
        s1h = s1b[:, c0:c0 + 8]
        s2h = s2b[:, c0:c0 + 8]
        rsh = rstd3[:, c0:c0 + 8]
        nmh = nmr3[:, c0:c0 + 8]
        ev_m3 = p.op("vector", lambda v, s1h=s1h: v.tensor_scalar_mul(
            s1h, s1h, 1.0 / D), waits=[ev_r3[2 * h + 1]])
        ev_va3 = p.op("vector", lambda v, rsh=rsh, s2h=s2h: v.tensor_scalar(
            rsh, s2h, 1.0 / D, EPS, ALU.mult, ALU.add),
            waits=[ev_sq3[c0 + 7]])
        ev_n03 = p.op("vector", lambda v, nmh=nmh, s1h=s1h:
                      v.scalar_tensor_tensor(
            nmh, s1h, -1.0, s1h, ALU.mult, ALU.mult), waits=[ev_m3])
        ev_ad3 = p.op("vector", lambda v, rsh=rsh, nmh=nmh: v.tensor_tensor(
            rsh, rsh, nmh, ALU.add), waits=[ev_va3, ev_n03])
        ev_rc3 = p.op("scalar", lambda sc, rsh=rsh: sc.activation(
            rsh, rsh, AF.Ln), waits=[ev_ad3])
        ev_sq3e = p.op("scalar", lambda sc, rsh=rsh: sc.activation(
            rsh, rsh, AF.Exp, scale=-0.5), waits=[ev_rc3])
        tail3[h] = p.op("vector", lambda v, nmh=nmh, s1h=s1h, rsh=rsh:
                        v.scalar_tensor_tensor(nmh, s1h, -1.0, rsh,
                                               ALU.mult, ALU.mult),
                        waits=[ev_sq3e, ev_n03])

    def emit_p6a(t):
        xt = x_tm[:, t * D:(t + 1) * D]
        ls = 2048 + (t % 2) * 1024
        lslot = opair[:, ls:ls + 1024]
        w = [tail3[t // 8]]
        if t < 2:
            w.append(("tp", p.cnt["tp"]))
        if t >= 2:
            w.append(ev_tr3[t - 2])
        ev_ap = p.op("scalar", lambda sc, o=lslot, sa=xt, t=t: sc.activation(
            o, sa, AF.Identity, scale=rstd3[:, t:t + 1],
            bias=nmr3[:, t:t + 1]), waits=w)
        tb = 6 + (t % 2)
        trw = [ev_ap, ev_wo_mm[NT - 1]]
        if t >= 2:
            trw.append(ev_ln3fm[t - 2])
        for dc in range(DC):
            ev_tr3[t] = p.op("tensor", lambda te, dc=dc, ls=ls, tb=tb:
                             te.transpose(
                bb(tb)[:, dc * 128:(dc + 1) * 128],
                in_=opair[:, ls + dc * 128: ls + (dc + 1) * 128],
                identity=identc),
                waits=trw if dc == 0 else (), inc="t" if dc == DC - 1 else None)
        dstv = lnR.rearrange("p (c n) -> p c n", c=8)[:, :, t * 128:(t + 1) * 128]
        srcv = bb(tb).rearrange("p (c n) -> p c n", c=8)
        ev_ln3fm[t] = p.op("vector", lambda v, d=dstv, s0=srcv: v.tensor_copy(
            d, s0), waits=[ev_tr3[t]], inc="v")

    def p7_mms(idx, tts, w):
        hc, e = idx // E, idx % E
        buf = ewb[:, (idx % 2) * 1024:(idx % 2) * 1024 + 1024]
        bs = 0 if e % 2 == 0 else 4
        for tt in tts:
            wtt = list(w) if tt == tts[0] else []
            if idx < 2:
                wtt.append(ev_ln3fm[4 * tt + 3])
            if idx == 0:
                wtt.append(ev_ln2ap[14 if tt < 2 else 15])
            if idx == 1 and tt >= 2:
                wtt.append(ev_ln3fm[12 + tt])
            for dc in range(DC):
                ev_ffn_pe[idx] = p.op("tensor", lambda te, dc=dc, tt=tt,
                                      bs=bs, buf=buf: te.matmul(
                    b(bs + tt), lhsT=buf[:, dc * 128:(dc + 1) * 128],
                    rhs=lnR[:, dc * 2048 + tt * 512: dc * 2048 + (tt + 1) * 512],
                    start=(dc == 0), stop=(dc == DC - 1)),
                    waits=wtt if dc == 0 else (),
                    inc="t" if (dc == DC - 1 and tt == 3) else None)

    ln3_tail(0)
    for t in range(8):
        emit_p6a(t)
    ev_ffn_pe = {}
    p7_mms(0, [0, 1], [ev_ew[0]])
    p5_tail(1)
    for tc in range(8, 16):
        p5_b(tc)
    for tc in range(8, 16):
        p5_c(tc)
    ln3_tail(1)
    for t in range(8, NT):
        emit_p6a(t)

    dbg("dbg_x1", x_tm, [ev_ln2[NT - 1]])
    dbg("dbg_ln3fm", lnR, [ev_ln3fm[NT - 1]])
    if stop == "p5":
        p.op("gpsimd", lambda g: g.nop(),
             waits=[("st", p.cnt["st"]), ("std", p.cnt["std"])])
        return _finish(nc, p)

    dbg("dbg_ln3fm", lnR, [ev_ln3fm[NT - 1]])

    ev_gmm = None
    for tc in range(NT):
        w = [ev_ln3fm[tc]] + ([ev_gw] if tc == 0 else [])
        first = True
        for dc in range(DC):
            ev_gmm = p.op("tensor", lambda te, dc=dc, tc=tc: te.matmul(
                b(4)[:, tc * E:(tc + 1) * E],
                lhsT=lnR[:, dc * 2048 + tc * 128: dc * 2048 + (tc + 1) * 128],
                rhs=gw[:, dc * E:(dc + 1) * E],
                start=(dc == 0), stop=(dc == DC - 1)),
                waits=w if first else (), inc="t" if dc == DC - 1 else None)
            first = False
    gts = fmst[:, 0:128]
    g3 = gts.rearrange("p (t e) -> p t e", e=E)
    ind = hsb[:, 1536:1792].bitcast(F32)
    i3 = ind.rearrange("p (t e) -> p t e", e=E)
    gb_b = gbr[:, None, :].broadcast_to([128, NT, E])
    ev_g0 = p.op("vector", lambda v: v.tensor_tensor(
        g3, b(4)[:, 0:128].rearrange("p (t e) -> p t e", e=E), gb_b, ALU.add),
        waits=[ev_gmm, ev_consts, ev_ln3fm[NT - 1]])
    ev_m1 = p.op("vector", lambda v: v.tensor_reduce(
        gsc[:, 0:16], g3, AX.X, ALU.max), waits=[ev_g0])
    m1_b = gsc[:, 0:16][:, :, None].broadcast_to([128, NT, E])
    ev_i1 = p.op("vector", lambda v: v.tensor_tensor(
        i3, g3, m1_b, ALU.is_ge), waits=[ev_m1])
    ev_st = p.op("vector", lambda v: v.scalar_tensor_tensor(
        i3, i3, -1e30, g3, ALU.mult, ALU.add), waits=[ev_i1])
    ev_m2 = p.op("vector", lambda v: v.tensor_reduce(
        gsc[:, 16:32], i3, AX.X, ALU.max), waits=[ev_st])
    m2_b = gsc[:, 16:32][:, :, None].broadcast_to([128, NT, E])
    ev_i2 = p.op("vector", lambda v: v.tensor_tensor(
        i3, g3, m2_b, ALU.is_ge), waits=[ev_m2])
    cstate = {}

    def emit_counts_head():
        ev_cntmm = p.op("tensor", lambda te: te.matmul(
            b(5)[0:1, 0:128], lhsT=onesc, rhs=ind,
            start=True, stop=True), waits=[ev_i2, ev_consts], inc="t")
        cnt_view = b(5)[0:1, 0:NT * E].rearrange("p (t e) -> p e t", e=E)
        cstate["cl"] = p.op("vector", lambda v: v.tensor_reduce(
            cnt_l, cnt_view, AX.X, ALU.add),
            waits=[ev_cntmm], inc="v")

    def emit_counts_tail():
        p.op("sync", lambda sy: sy.dma_start(out=d_ccin, in_=cnt_l),
             waits=[cstate["cl"]], inc="ccs", amt=16)
        if cc_mode == "ar":
            p.op("gpsimd", lambda g: g.collective_compute(
                "AllReduce", ALU.add, replica_groups=[list(range(8))],
                ins=[d_ccin], outs=[d_ccout]), waits=[("ccs", 16)],
                inc="cca", amt=1)
            p.op("sync", lambda sy: sy.dma_start(
                out=cb, in_=d_ccout[0:1, :].broadcast_to([128, E])),
                waits=[("cca", 1)], inc="ccs", amt=16)
        else:
            p.op("sync", lambda sy: sy.dma_start(
                out=cb, in_=d_ccin[0:1, :].broadcast_to([128, E])),
                waits=[("ccs", 16)], inc="ccs", amt=16)
        ev_cb = p.op("vector", lambda v: v.tensor_scalar_mul(
            cb, cb, TOTAL_INV), waits=[("ccs", 32)])
        ev_ebs = p.op("vector", lambda v: v.tensor_tensor(
            ebs.rearrange("p (h2 e) -> p h2 e", e=E),
            eb.rearrange("p (h2 e) -> p h2 e", e=E),
            cb[:, None, :].broadcast_to([128, 8, E]), ALU.mult),
            waits=[ev_cb, ev_consts], inc="v")
        cstate["cb"] = ev_cb
        cstate["ebs"] = ev_ebs

    if stop == "p6":
        emit_counts_head()
        emit_counts_tail()
        dbg("dbg_cb", cb, [cstate["cb"]])
        p.op("gpsimd", lambda g: g.nop(),
             waits=[("st", p.cnt["st"]), ("std", p.cnt["std"])])
        return _finish(nc, p)

    acc = f32s.bitcast(BF16)[:, 0:2048]
    ffn_fm = v_tm
    ev_ffn_s = {}
    ev_add = {}
    HC = 8
    rbuf = {}
    for e in range(E):
        if e < 4:
            rbuf[e] = qkf_tm[:, e * 2048:(e + 1) * 2048]
        elif e < 6:
            rbuf[e] = qkf_fm[:, (e - 4) * 2048:(e - 4) * 2048 + 2048]
        else:
            rbuf[e] = xn_qk[:, (e - 6) * 2048:(e - 6) * 2048 + 2048]
    gate7vg = p.counts_event("v", "g")
    tq_all = [("tq0", p.cnt["tq0"]), ("tq1", p.cnt["tq1"]),
              ("tq2", p.cnt["tq2"]), ("tq3", p.cnt["tq3"])]
    for hc in range(HC):
        for e in range(E):
            idx = hc * E + e
            w = [ev_ew[idx]]
            if idx >= 2:
                w.append(ev_ffn_s[idx - 2])
            if idx == 1:
                w += [ev_g0, cstate["cl"]]
            bs = 0 if e % 2 == 0 else 4
            if idx == 0:
                p7_mms(0, [2, 3], [])
                emit_counts_head()
            else:
                p7_mms(idx, [0, 1, 2, 3], w)
            if idx + 2 < 64:
                ev_ew[idx + 2] = p.op("gpsimd", lambda g, idx=idx: g.dma_start(
                    out=ewb[:, (idx % 2) * 1024:(idx % 2) * 1024 + 1024],
                    in_=d_ew[idx + 2]), waits=[ev_ffn_pe[idx]],
                    inc=f"ldew{idx % 2}", amt=16)
            if idx == 1:
                emit_counts_tail()
            if hc == 0:
                sw = [ev_ffn_pe[idx]]
                if idx == 0:
                    sw += tq_all + gate7vg
                ev_ffn_s[idx] = p.op("scalar", lambda sc, e=e, bs=bs:
                                     sc.activation(
                    rbuf[e], psum[:, bs * 512: bs * 512 + 2048], AF.Relu,
                    bias=eb[:, e:e + 1]), waits=sw)
                if e == E - 1:
                    cw = [cstate["ebs"], ev_ffn_s[0]]
                    cw += [ev_fm_ap[i] for i in (12, 13, 14, 15)]
                    ev_cmb = p.op("vector", lambda v: v.tensor_scalar_mul(
                        acc, rbuf[0], cb[:, 0:1]), waits=cw)
                    for ce in range(1, E):
                        dst_c = ffn_fm[:, 0:2048] if ce == E - 1 else acc
                        ev_cmb = p.op("vector", lambda v, ce=ce, dst_c=dst_c:
                                      v.scalar_tensor_tensor(
                            dst_c, rbuf[ce], cb[:, ce:ce + 1], acc,
                            ALU.mult, ALU.add),
                            waits=[ev_cmb, ev_ffn_s[ce]])
                    cstate["comb"] = ev_cmb
                continue
            sw = [ev_ffn_pe[idx], cstate["ebs"]]
            if idx == 9:
                sw += [ev_s2red[14], ev_s2red[15]]
            if e == 0:
                sw.append(cstate["comb"] if hc == 1 else ev_add[hc * E - 1])
            if e > 0 and idx >= 2:
                prev_add = ev_add.get(idx - 2)
                if prev_add is not None:
                    sw.append(prev_add)
            if e == 0:
                dsta = acc
            else:
                dsta = hsb[:, (idx % 2) * 2048:(idx % 2) * 2048 + 2048]
            ev_ffn_s[idx] = p.op("scalar", lambda sc, dsta=dsta, bs=bs,
                                 hc=hc, e=e: sc.activation(
                dsta, psum[:, bs * 512: bs * 512 + 2048], AF.Relu,
                scale=cb[:, e:e + 1],
                bias=ebs[:, hc * E + e: hc * E + e + 1]), waits=sw)
            if e > 0:
                dsta = (ffn_fm[:, hc * 2048:(hc + 1) * 2048] if e == E - 1
                        else acc)
                aw = [ev_ffn_s[idx], ev_add.get(idx - 1)]
                if aw[-1] is None:
                    aw = aw[:-1]
                if e == 1 and hc == 1:
                    aw.append(cstate["comb"])
                hsl = hsb[:, (idx % 2) * 2048:(idx % 2) * 2048 + 2048]
                ev_add[idx] = p.op(
                    "vector", lambda ve, dsta=dsta, hsl=hsl:
                    ve.tensor_tensor(dsta, acc, hsl, ALU.add),
                    waits=aw)

    dbg("dbg_ffnfm", v_tm[:, 0:16384], [ev_add[HC * E - 1]])

    gate8 = p.counts_event("v", "s", "g")
    base4 = 3 * 64
    s1a4 = stats[:, base4:base4 + 16]
    s2a4 = stats[:, base4 + 16:base4 + 32]
    rstd4 = stats[:, base4 + 32:base4 + 48]
    nmr4 = stats[:, base4 + 48:base4 + 64]
    ev_tr4 = {}
    ev_r4 = {}
    ev_q4 = {}
    ev_ap4 = {}
    ev_add4 = {}
    tail4 = {}

    def p8_a(t):
        tb = t % 8
        w = list(gate8) if t == 0 else []
        if t >= 8:
            w.append(ev_ap4[t - 8])
        for dc in range(DC):
            ev_tr4[t] = p.op("tensor", lambda te, dc=dc, t=t, tb=tb:
                             te.transpose(
                bb(tb)[:, dc * 128:(dc + 1) * 128],
                in_=ffn_fm[:, dc * 2048 + t * 128: dc * 2048 + (t + 1) * 128],
                identity=identc),
                waits=w if dc == 0 else (), inc="t" if dc == DC - 1 else None)
        rw = [ev_tr4[t]] + (list(gate8) if t == 0 else [])
        ev_r4[t] = p.op("vector", lambda v, t=t, tb=tb: v.tensor_reduce(
            s1a4[:, t:t + 1], bb(tb), AX.X, ALU.add), waits=rw)
        qw = [ev_r4[t]] + list(sq_war()) + (list(gate8) if t == 0 else [])
        ev_q4[t] = p.op("scalar", lambda sc, t=t, tb=tb: sc.activation(
            sqj[:, 0:1024], bb(tb), AF.Square, accum_out=s2a4[:, t:t + 1]),
            waits=qw)
        sq_last[0] = ev_q4[t]

    def p8_tail(h):
        c0 = h * 4
        s1h = s1a4[:, c0:c0 + 4]
        s2h = s2a4[:, c0:c0 + 4]
        rsh = rstd4[:, c0:c0 + 4]
        nmh = nmr4[:, c0:c0 + 4]
        ev_m = p.op("vector", lambda v, s1h=s1h: v.tensor_scalar_mul(
            s1h, s1h, 1.0 / D), waits=[ev_r4[c0 + 3]])
        ev_va = p.op("vector", lambda v, rsh=rsh, s2h=s2h: v.tensor_scalar(
            rsh, s2h, 1.0 / D, EPS, ALU.mult, ALU.add), waits=[ev_q4[c0 + 3]])
        ev_n0 = p.op("vector", lambda v, nmh=nmh, s1h=s1h: v.scalar_tensor_tensor(
            nmh, s1h, -1.0, s1h, ALU.mult, ALU.mult), waits=[ev_m])
        ev_ad = p.op("vector", lambda v, rsh=rsh, nmh=nmh: v.tensor_tensor(
            rsh, rsh, nmh, ALU.add), waits=[ev_va, ev_n0])
        ev_rc = p.op("scalar", lambda sc, rsh=rsh: sc.activation(
            rsh, rsh, AF.Ln), waits=[ev_ad])
        ev_sq = p.op("scalar", lambda sc, rsh=rsh: sc.activation(
            rsh, rsh, AF.Exp, scale=-0.5), waits=[ev_rc])
        tail4[h] = p.op("vector", lambda v, nmh=nmh, s1h=s1h, rsh=rsh:
                        v.scalar_tensor_tensor(nmh, s1h, -1.0, rsh,
                                               ALU.mult, ALU.mult),
                        waits=[ev_sq, ev_n0])

    def p8_b(t):
        tb = t % 8
        tmp = xn_qk[:, tb * 1024:(tb + 1) * 1024]
        aw = [tail4[t // 4]]
        if t == 0:
            aw += gate8
        if t >= 8:
            aw.append(ev_add4[t - 8])
        ev_ap4[t] = p.op("scalar", lambda sc, t=t, tb=tb, tmp=tmp:
                         sc.activation(
            tmp, bb(tb), AF.Identity, scale=rstd4[:, t:t + 1],
            bias=nmr4[:, t:t + 1]), waits=aw)
        xsl = x_tm[:, t * D:(t + 1) * D]
        if t % 2 == 0:
            ev_add4[t] = p.op("vector", lambda v, xsl=xsl, tmp=tmp:
                              v.tensor_tensor(xsl, xsl, tmp, ALU.add),
                              waits=[ev_ap4[t]])
            p.op("gpsimd", lambda g, t=t, xsl=xsl: g.dma_start(
                out=d_out[t * 128:(t + 1) * 128, :], in_=xsl),
                waits=[ev_add4[t]], inc="st", amt=16)
        else:
            fsl = lnR[:, (t // 2) * 2048:(t // 2) * 2048 + 2048].bitcast(F32)
            ev_add4[t] = p.op("vector", lambda v, xsl=xsl, tmp=tmp, fsl=fsl:
                              v.tensor_tensor(fsl, xsl, tmp, ALU.add),
                              waits=[ev_ap4[t]])
            p.op("sync", lambda sy, t=t, fsl=fsl: sy.dma_start(
                out=d_out[t * 128:(t + 1) * 128, :], in_=fsl),
                waits=[ev_add4[t]], inc="st2", amt=16)

    for t in range(8):
        p8_a(t)
    p8_tail(0)
    for t in range(4):
        p8_b(t)
    for t in range(8, 12):
        p8_a(t)
    p8_tail(1)
    for t in range(4, 8):
        p8_b(t)
    for t in range(12, 16):
        p8_a(t)
    p8_tail(2)
    for t in range(8, 12):
        p8_b(t)
    p8_tail(3)
    for t in range(12, 16):
        p8_b(t)

    p.op("gpsimd", lambda g: g.nop(),
         waits=[("st", p.cnt["st"]), ("std", p.cnt["std"]),
                ("st2", p.cnt["st2"])])
    return _finish(nc, p)


def _finish(nc, p):
    from contextlib import ExitStack
    es = ExitStack()
    semh = {name: es.enter_context(nc.semaphore(name)) for name in p.cnt}
    block = es.enter_context(nc.Block())

    @block.sync
    def _(sync):
        _emit(sync, p.ops["sync"], semh)

    @block.tensor
    def _(tensor):
        _emit(tensor, p.ops["tensor"], semh)

    @block.vector
    def _(vector):
        _emit(vector, p.ops["vector"], semh)

    @block.scalar
    def _(scalar):
        _emit(scalar, p.ops["scalar"], semh)

    @block.gpsimd
    def _(gpsimd):
        _emit(gpsimd, p.ops["gpsimd"], semh)

    es.close()
    return nc


def stage_inputs(inputs):
    x_all = np.ascontiguousarray(np.asarray(inputs["L0_attn_hidden_states"]))
    wq = np.asarray(inputs["L0_attn_Wq"])
    wk = np.asarray(inputs["L0_attn_Wk"])
    wv = np.asarray(inputs["L0_attn_Wv"])
    wo_ = np.asarray(inputs["L0_attn_Wo"])
    gamma = np.asarray(inputs["L0_attn_gamma"])
    beta = np.asarray(inputs["L0_attn_beta"])
    gw_ = np.asarray(inputs["L0_ffn_gate_w"])
    gb_ = np.asarray(inputs["L0_ffn_gate_b"])
    ew_ = np.asarray(inputs["L0_ffn_expert_w"])
    eb_ = np.asarray(inputs["L0_ffn_expert_b"])

    wqkv = np.ascontiguousarray(np.concatenate([wq, wk, wv], axis=0).T)
    wo_t = np.ascontiguousarray(wo_.T)
    gw_t = np.ascontiguousarray(gw_.T)
    ew_s = np.ascontiguousarray(
        ew_.reshape(E, 8, 128, 8, 128).transpose(1, 0, 4, 3, 2)
        .reshape(64, 128, 1024))
    eb_s = np.ascontiguousarray(
        eb_.T.reshape(8, 128, E).transpose(1, 0, 2).reshape(128, 64))
    gbr = np.ascontiguousarray(np.broadcast_to(gb_[None, :], (128, E)))
    gmh = np.ascontiguousarray(np.broadcast_to(
        np.tile(gamma, 32)[None, :], (128, 512)).astype(np.float32))
    bth = np.ascontiguousarray(np.broadcast_to(
        np.tile(beta, 32)[None, :], (128, 512)).astype(np.float32))
    maskc = np.triu(np.ones((128, 128), dtype=np.float32)).astype(
        ml_dtypes.bfloat16)
    identc = np.eye(128, dtype=np.float32).astype(ml_dtypes.bfloat16)
    onesc = np.ones((128, 1), dtype=np.float32)
    onesr = np.ones((1, 128), dtype=np.float32)

    shared = dict(wqkv=wqkv, wo=wo_t, gw=gw_t, ew=ew_s, eb=eb_s, gbr=gbr,
                  gmh=gmh, bth=bth, maskc=maskc, identc=identc,
                  onesc=onesc, onesr=onesr)
    in_maps = []
    for bi in range(8):
        m = dict(shared)
        m["x"] = np.ascontiguousarray(x_all[bi])
        in_maps.append(m)
    return in_maps


def _run(inputs, debug=(), stop=None):
    nc = build(debug=debug, stop=stop)
    in_maps = stage_inputs(inputs)
    return run_bass_kernel_spmd(nc, in_maps, core_ids=list(range(8)))


def kernel(**inputs):
    res = _run(inputs, debug=DEBUG, stop=STOP)
    out = np.stack([np.asarray(res.results[i]["out"]) for i in range(8)], axis=0)
    return np.ascontiguousarray(out.astype(np.float32))
